# version 06 (final)
# speedup vs baseline: 1.2508x; 1.0263x over previous
import sys
sys.path.insert(0, "/opt/trn_rl_repo")
from contextlib import ExitStack

import numpy as np
import ml_dtypes

import concourse.bass as bass
import concourse.mybir as mybir
from concourse.bass_utils import run_bass_kernel_spmd

F32 = mybir.dt.float32
BF16 = mybir.dt.bfloat16
AF = mybir.ActivationFunctionType
ALU = mybir.AluOpType
AX = mybir.AxisListType

D = 1024
T = 2048
NT = T // 128
ND = D // 128
NH = 16
DH = 64
NCORES = 8

BF = ml_dtypes.bfloat16



class _MockInst:
    def then_inc(self, *a, **k):
        return self


_MOCK_INST = _MockInst()

_NOCHAIN = {"wait_ge", "dma_start", "dma_start_transpose", "nop", "sem_inc",
            "drain", "engine_nop"}


class EngineProxy:

    def __init__(self, p, role, real, chain_sem_name):
        self._p = p
        self._role = role
        self._real = real
        self._ch = chain_sem_name

    def wait_ge(self, sem, v):
        if self._real is not None:
            self._real.wait_ge(sem, v)

    def dma_start(self, *a, **k):
        p = self._p
        n = p._cnt.get(self._ch, 0)
        if self._real is not None:
            if n > 0:
                self._real.wait_ge(p._sems[self._ch], n)
            return self._real.dma_start(*a, **k)
        return _MOCK_INST

    def collective_compute(self, *a, **k):
        p = self._p
        n = p._cnt.get(self._ch, 0)
        if self._real is not None:
            if n > 0:
                self._real.wait_ge(p._sems[self._ch], n)
            return self._real.collective_compute(*a, **k)
        return _MOCK_INST

    def nop(self):
        if self._real is not None:
            return self._real.nop()
        return _MOCK_INST

    def __getattr__(self, name):
        p = self._p

        def f(*a, **k):
            if self._ch is None:
                if self._real is not None:
                    return getattr(self._real, name)(*a, **k)
                return _MOCK_INST
            n = p._cnt[self._ch]
            if self._real is not None:
                if n > 0:
                    self._real.wait_ge(p._sems[self._ch], n)
                inst = getattr(self._real, name)(*a, **k)
                inst.then_inc(p._sems[self._ch], 1)
            else:
                inst = _MOCK_INST
            p._cnt[self._ch] += 1
            return inst
        return f


class P:

    def __init__(self, nc, role, real_engine, sems):
        self.nc = nc
        self.role = role
        self._sems = sems
        self._cnt = {k: 0 for k in sems}
        self.pe = EngineProxy(self, 'pe', real_engine if role == 'pe' else None, None)
        self.v = EngineProxy(self, 'v', real_engine if role == 'v' else None, 'ch_v')
        self.a = EngineProxy(self, 'a', real_engine if role == 'a' else None, 'ch_a')
        self.g = EngineProxy(self, 'g', real_engine if role == 'g' else None, 'ch_g')
        self.y = EngineProxy(self, 'y', real_engine if role == 'y' else None, None)
        self._real = real_engine

    def inc(self, inst, sem, n=1):
        if inst is not _MOCK_INST and inst is not None:
            inst.then_inc(self._sems[sem], n)
        self._cnt[sem] += n
        return self._cnt[sem]

    def val(self, sem):
        return self._cnt[sem]

    def wait(self, role, sem, v=None):
        if v is None:
            v = self._cnt[sem]
        if v <= 0:
            return
        if self.role == role:
            self._real.wait_ge(self._sems[sem], v)


def run_script(nc, script, sem_names):
    with ExitStack() as st:
        sems = {name: st.enter_context(nc.semaphore("sem_" + name))
                for name in sem_names}
        block = st.enter_context(nc.Block())

        @block.sync
        def _(y):
            script(P(nc, 'y', y, sems))

        @block.gpsimd
        def _(g):
            script(P(nc, 'g', g, sems))

        @block.vector
        def _(v):
            script(P(nc, 'v', v, sems))

        @block.tensor
        def _(t):
            script(P(nc, 'pe', t, sems))

        @block.scalar
        def _(a):
            script(P(nc, 'a', a, sems))


SEM_NAMES = ['dma', 'dy', 'dz', 'dcs', 'wda', 'wdb', 'gd', 'dxf', 'kda', 'kdb',
             'vda', 'vdb', 'ch_v', 'ch_a', 'ch_g', 'tp', 'mm', 'scm', 'om',
             'sm', 'cc', 'wd', 'xd']



def host_consts():
    ii = np.arange(128)
    c = {}
    c["c_id"] = np.eye(128, dtype=np.float32)
    mT = (ii[:, None] <= ii[None, :]).astype(np.float32)
    c["c_maskT4"] = np.tile(mT, (1, 4))
    c["c_ones_row"] = np.ones((1, 128), dtype=np.float32)
    c["c_zrow"] = np.zeros((1, 512), dtype=np.float32)
    c["c_ones512"] = np.ones((1, 512), dtype=np.float32)
    i64 = np.eye(64, dtype=np.float32)
    pfull = np.block([[i64, i64], [i64, i64]]).astype(np.float32)
    pdiag = np.block([[i64, i64], [np.zeros((64, 64), np.float32), i64]]).astype(np.float32)
    c["c_pcum"] = np.concatenate([pdiag, pfull], axis=1)
    blk = np.zeros((128, 8 * 16), dtype=np.float32)
    for dt in range(8):
        blk[0:64, dt * 16 + 2 * dt] = 1.0
        blk[64:128, dt * 16 + 2 * dt + 1] = 1.0
    c["c_blkdt"] = blk
    jj = np.arange(128)
    c["c_amask_tril"] = np.where(jj[:, None] >= jj[None, :], 0.0, -1e9).astype(np.float32)
    c["c_amask_triu"] = np.where(jj[:, None] <= jj[None, :], 0.0, -1e9).astype(np.float32)
    perm = np.zeros((128, 128), dtype=np.float32)
    for blk_ in range(2):
        for dd in range(64):
            src_ = blk_ * 64 + (dd + 32) % 64
            perm[src_, blk_ * 64 + dd] = 1.0
    c["c_perm"] = perm
    sh = np.zeros((128, 128), dtype=np.float32)
    for tt_ in range(1, 128):
        sh[tt_ - 1, tt_] = 1.0
    c["c_shiftp"] = sh
    cor = np.zeros((128, 128), dtype=np.float32)
    cor[127, 0] = 1.0
    c["c_corner"] = cor
    pat2 = np.zeros((2, 128), dtype=np.float32)
    pat2[0, 0:64] = 1.0
    pat2[1, 64:128] = 1.0
    c["c_pat2"] = pat2
    blk2 = np.zeros((128, 2), dtype=np.float32)
    blk2[0:64, 0] = 1.0
    blk2[64:128, 1] = 1.0
    c["c_blk2"] = blk2
    c["c_onesf"] = np.ones((1, 128), dtype=np.float32)
    DIM_HEAD, WINDOW, n = 64, 512, 2048
    half = np.arange(0, DIM_HEAD, 2, dtype=np.float64)
    inv_freq = 1.0 / 10000.0 ** (half / DIM_HEAD)
    tpos = np.arange(n, dtype=np.float64)
    freqs = tpos[:, None] * inv_freq[None, :]
    freqs = np.concatenate([freqs, freqs], axis=-1)
    base = (half + 0.4 * DIM_HEAD) / (1.4 * DIM_HEAD)
    power = (tpos - n // 2) / float(WINDOW // 2)
    scale = base[None, :] ** power[:, None]
    scale = np.concatenate([scale, scale], axis=-1)
    cos = np.cos(freqs)
    sin = np.sin(freqs)
    sgn = np.concatenate([-np.ones(32), np.ones(32)])
    aq = (cos * scale).T
    bq = (sin * scale).T * sgn[:, None]
    ak = (cos / scale).T
    bk = (sin / scale).T * sgn[:, None]
    for nm, arr in (("c_aq", aq), ("c_bq", bq), ("c_ak", ak), ("c_bk", bk)):
        c[nm] = np.concatenate([arr, arr], axis=0).astype(np.float32)
    return c



def emit_ln_stats(p, srcs, BN, MV, RSTD, NMR, eps=1e-5):
    v, a = p.v, p.a
    for k, s in enumerate(srcs):
        v.bn_stats(BN[:, 6 * k:6 * k + 6], s)
    v.bn_aggr(MV[:], BN[:, 0:6 * len(srcs)])
    v.tensor_scalar(RSTD[:], MV[:, 1:2], eps, None, op0=ALU.add)
    v.reciprocal(RSTD[:], RSTD[:])
    p.wait('a', 'ch_v')
    p.a.activation(out=RSTD[:], in_=RSTD[:], func=AF.Sqrt)
    p.wait('v', 'ch_a')
    i = v.scalar_tensor_tensor(NMR[:], MV[:, 0:1], -1.0, RSTD[:],
                               op0=ALU.mult, op1=ALU.mult)
    return i



def emit_l0_attn(p, ap, sb_, upto=99):
    pe, v, a, g, y = p.pe, p.v, p.a, p.g, p.y
    X = sb_['X']; W = sb_['W']; LNb = sb_['LNb']; LT = sb_['LT']
    QF = sb_['QF']; KF = sb_['KF']; KT = sb_['KT']; VT = sb_['VT']
    KCT = sb_['KCT']
    QT = sb_['OT']
    OT = sb_['OT']; OTT = sb_['OTT']; Ssb = sb_['Ssb']; ATs = sb_['ATs']
    T1 = sb_['T1']; T2 = sb_['T2']; BN = sb_['BN']; MV = sb_['MV']
    RSTD = sb_['RSTD']; NMR = sb_['NMR']; ZRall = sb_['ZRall']
    IDb = sb_['IDb']; MT4 = sb_['MT4']; PCUM = sb_['PCUM']
    ONESr = sb_['ONESr']; BOb = sb_['BOb']; ZROW = sb_['ZROW']
    Sf32 = sb_['Sf32']
    B = sb_['B']
    PS_Sa, PS_Sb = sb_['PS_Sa'], sb_['PS_Sb']

    xr = ap['x'].rearrange("(t p) d -> t p d", p=128)
    for t in range(NT):
        i = y.dma_start(out=X[:, t * D:(t + 1) * D], in_=xr[t])
        p.inc(i, 'dy', 16)
    for n in ("q", "k", "v", "o"):
        wr = ap['W' + n].rearrange("(a p) d -> a p d", p=128)
        for di in range(ND):
            i = g.dma_start(out=W[n][:, di * D:(di + 1) * D], in_=wr[di])
            p.inc(i, 'dma', 16)
    i = g.dma_start(out=IDb[:], in_=ap['c_id']); p.inc(i, 'dma', 16)
    i = g.dma_start(out=MT4[:], in_=ap['c_maskT4']); p.inc(i, 'dma', 16)
    i = g.dma_start(out=PCUM[:], in_=ap['c_pcum']); p.inc(i, 'dma', 16)
    i = g.dma_start(out=ONESr[:], in_=ap['c_ones_row']); p.inc(i, 'dma', 16)
    i = g.dma_start(out=ZROW[:], in_=ap['c_zrow']); p.inc(i, 'dma', 16)
    i = g.dma_start(out=BOb[:], in_=ap['bo']); p.inc(i, 'dma', 16)
    all_loaded = p.val('dma')

    if upto >= 0:
        p.wait('v', 'dma', all_loaded)
        p.wait('v', 'dy', p.val('dy'))
        v.memset(Ssb[:], 0.0)
        v.memset(Sf32[:], 0.0)
        p.wait('pe', 'ch_v')
        p.wait('pe', 'dma', all_loaded)
        p.wait('pe', 'dy', p.val('dy'))
        p.wait('a', 'dma', all_loaded)

    import os
    nsup = int(os.environ.get("L0_NSUP", "4"))
    ntile = int(os.environ.get("L0_NTILE", "4"))
    for s in range(nsup):
        if upto < 1:
            break
        for tt in range(ntile):
            t = s * 4 + tt
            xt = X[:, t * D:(t + 1) * D]
            emit_ln_stats(p, [xt[:, 0:512], xt[:, 512:1024]], BN, MV, RSTD, NMR)
            v.tensor_scalar(LNb[:], xt, RSTD[:], NMR[:],
                            op0=ALU.mult, op1=ALU.add)
            p.wait('pe', 'ch_v')
            for d in range(ND):
                i = pe.transpose(B[5][:, d * 64:(d + 1) * 64].bitcast(BF16),
                                 LNb[:, d * 128:(d + 1) * 128], IDb[:])
            p.inc(i, 'tp')
            p.wait('v', 'tp')
            v.tensor_copy(
                LT[:].rearrange("p (d c) -> p d c", c=512)[:, :, tt * 128:(tt + 1) * 128],
                B[5][:, 0:512].bitcast(BF16).rearrange("p (d c) -> p d c", c=128))
            p.wait('pe', 'ch_v')

        if upto < 2:
            continue
        def feat_evac(bank, dst):
            v.tensor_scalar(T1[:], bank[:], 0.0, None, op0=ALU.min)
            p.wait('a', 'ch_v')
            a.activation(out=T2[:], in_=T1[:], func=AF.Exp)
            p.wait('v', 'ch_a')
            v.scalar_tensor_tensor(dst, bank[:], 0.0, T2[:],
                                   op0=ALU.max, op1=ALU.add)

        pingpong = [B[5], B[2]]
        ev_after_fill = []

        def fill_guard():
            fi = len(ev_after_fill)
            if fi >= 2:
                p.wait('pe', 'ch_v', ev_after_fill[fi - 2])
            return pingpong[fi % 2]

        for (wname, dstF) in (("q", QF), ("k", KF)):
            for do in range(ND):
                bank = fill_guard()
                for di in range(ND):
                    i = pe.matmul(bank[:],
                                  W[wname][:, di * D + do * 128: di * D + do * 128 + 128],
                                  LT[:, di * 512:(di + 1) * 512],
                                  start=(di == 0), stop=(di == ND - 1))
                p.inc(i, 'mm')
                p.wait('v', 'mm')
                feat_evac(bank, dstF[:, do * 512:(do + 1) * 512])
                ev_after_fill.append(p.val('ch_v'))
        for (wname, dstT, feat) in (("k", KT, True), ("v", VT, False)):
            for tt in range(4):
                for dh in range(2):
                    bank = fill_guard()
                    for di in range(ND):
                        i = pe.matmul(bank[:],
                                      LT[:, di * 512 + tt * 128: di * 512 + tt * 128 + 128],
                                      W[wname][:, di * D + dh * 512: di * D + (dh + 1) * 512],
                                      start=(di == 0), stop=(di == ND - 1))
                    p.inc(i, 'mm')
                    p.wait('v', 'mm')
                    dst = dstT[:, tt * D + dh * 512: tt * D + (dh + 1) * 512]
                    if feat:
                        feat_evac(bank, dst)
                    else:
                        v.tensor_copy(dst, bank[:])
                    ev_after_fill.append(p.val('ch_v'))

        for tt in range(4):
            bank = fill_guard()
            p.wait('pe', 'ch_v', ev_after_fill[7])
            for d in range(ND):
                i = pe.transpose(bank[:, d * 64:(d + 1) * 64].bitcast(BF16),
                                 QF[:].rearrange("p (d c) -> p d c", c=512)[:, d, tt * 128:(tt + 1) * 128],
                                 IDb[:])
            p.inc(i, 'tp')
            p.wait('v', 'tp')
            v.tensor_copy(
                QT[:, tt * D:(tt + 1) * D].rearrange("p (d c) -> p d c", c=128),
                bank[:, 0:512].bitcast(BF16).rearrange("p (d c) -> p d c", c=128))
            ev_after_fill.append(p.val('ch_v'))

        for tt in range(4):
            for dh in range(2):
                bank = fill_guard()
                djs = (4 * dh, 4 * dh + 4)
                for dj in range(djs[0], djs[1]):
                    for di in range(dj + 1):
                        pat = PCUM[:, 0:128] if di == dj else PCUM[:, 128:256]
                        i = pe.matmul(bank[:, (dj - djs[0]) * 128:(dj - djs[0] + 1) * 128],
                                      KF[:, di * 512 + tt * 128: di * 512 + tt * 128 + 128],
                                      pat,
                                      start=(di == 0), stop=(di == dj))
                p.inc(i, 'mm')
                p.wait('v', 'mm')
                v.tensor_copy(KCT[:, tt * D + dh * 512: tt * D + (dh + 1) * 512],
                              bank[:])
                ev_after_fill.append(p.val('ch_v'))

        for tt in range(4):
            v.tensor_tensor(T1[:].rearrange("p (g f) -> p g f", g=8) if False else T1[:],
                            QT[:, tt * D: tt * D + 512], KCT[:, tt * D: tt * D + 512],
                            op=ALU.mult)
            i = v.tensor_tensor(T2[:], QT[:, tt * D + 512: (tt + 1) * D],
                                KCT[:, tt * D + 512: (tt + 1) * D], op=ALU.mult)
            v.tensor_reduce(ZRall[:, tt * 16: tt * 16 + 8].unsqueeze(2),
                            T1[:].rearrange("p (g f) -> p g f", g=8),
                            axis=AX.X, op=ALU.add)
            v.tensor_reduce(ZRall[:, tt * 16 + 8: tt * 16 + 16].unsqueeze(2),
                            T2[:].rearrange("p (g f) -> p g f", g=8),
                            axis=AX.X, op=ALU.add)
            v.tensor_scalar(ZRall[:, tt * 16:(tt + 1) * 16],
                            ZRall[:, tt * 16:(tt + 1) * 16], 1e-6, None, op0=ALU.add)
            v.reciprocal(ZRall[:, tt * 16:(tt + 1) * 16],
                         ZRall[:, tt * 16:(tt + 1) * 16])
        p.wait('pe', 'ch_v')

        if upto < 3:
            continue
        scan_lv = int(os.environ.get("L0_SCAN", "9"))
        if s == 0:
            mk_mark = [0]
            oz_mark = [0]
            scp_mark = [0]
        for c4 in range(4):
            c = s * 4 + c4
            for w in range(4):
                h0 = w * 4
                pp = w % 2
                bE, bO = B[2 * pp], B[2 * pp + 1]
                at_s = ATs[:, pp * 512:(pp + 1) * 512]

                p.wait('pe', 'ch_v', mk_mark[0])
                for k in range(4):
                    h = h0 + k
                    qs_ = _fm_head(QF, h, c4)
                    ks_ = _fm_head(KF, h, c4)
                    bnk = bE if k % 2 == 0 else bO
                    i = pe.matmul(bnk[:, (k // 2) * 128:(k // 2) * 128 + 128],
                                  ks_, qs_, start=True, stop=True)
                p.inc(i, 'scm')

                p.wait('v', 'scm')
                at3 = at_s.rearrange("p (k f) -> p k f", k=4)
                m3 = MT4[:, 0:256].rearrange("p (k f) -> p k f", k=2)
                v.tensor_tensor(at3[:, 0::2, :],
                                bE[:, 0:256].rearrange("p (k f) -> p k f", k=2),
                                m3, op=ALU.mult)
                v.tensor_tensor(at3[:, 1::2, :],
                                bO[:, 0:256].rearrange("p (k f) -> p k f", k=2),
                                m3, op=ALU.mult)
                mk_mark[0] = p.val('ch_v')
                if scan_lv < 2:
                    continue

                p.wait('pe', 'ch_v', mk_mark[0])
                p.wait('pe', 'ch_v', oz_mark[0])
                p.wait('pe', 'ch_v', scp_mark[0])
                for k in range(4):
                    h = h0 + k
                    pe.matmul(_po(B)[:, k * 64:(k + 1) * 64],
                              at_s[:, k * 128:(k + 1) * 128],
                              VT[:, c4 * D + h * 64: c4 * D + (h + 1) * 64],
                              start=(k == 0), stop=False)
                if os.environ.get("L0_NOI2"):
                    i = pe.matmul(_po(B)[:, 192:256],
                                  at_s[:, 3 * 128:4 * 128],
                                  VT[:, c4 * D + (h0 + 3) * 64: c4 * D + (h0 + 4) * 64],
                                  start=False, stop=True)
                else:
                    for pr in range(2):
                        g2 = h0 // 2 + pr
                        i = pe.matmul(_po(B)[:, pr * 128:(pr + 1) * 128],
                                      QF[:, g2 * 512 + c4 * 128: g2 * 512 + (c4 + 1) * 128],
                                      Ssb[:, g2 * 128:(g2 + 1) * 128],
                                      start=False, stop=(pr == 1))
                p.inc(i, 'om')

                if scan_lv >= 3:
                    p.wait('pe', 'ch_v', scp_mark[0])
                    for k in range(4):
                        h = h0 + k
                        PSb = PS_Sa if h % 2 == 0 else PS_Sb
                        part = (h % 2) * 64
                        colg = (h // 2) * 64
                        i = pe.matmul(PSb[part:part + 64, colg:colg + 64],
                                      KT[:, c4 * D + h * 64: c4 * D + (h + 1) * 64],
                                      VT[:, c4 * D + h * 64: c4 * D + (h + 1) * 64],
                                      start=True, stop=True)
                    p.inc(i, 'sm')

                p.wait('v', 'om')
                v.tensor_tensor(
                    OT[:, c4 * D + h0 * 64: c4 * D + (h0 + 4) * 64].rearrange(
                        "p (k f) -> p k f", k=4),
                    _po_vals(_po(B)),
                    ZRall[:, c4 * 16 + h0: c4 * 16 + h0 + 4].unsqueeze(2)
                          .broadcast_to([128, 4, 64]),
                    op=ALU.mult)
                oz_mark[0] = p.val('ch_v')
                if scan_lv >= 3:
                    p.wait('v', 'sm')
                    for pr in range(2):
                        g2 = h0 // 2 + pr
                        v.tensor_tensor(_sdiag(Sf32, g2, 0), _sdiag(Sf32, g2, 0),
                                        _psslice(PS_Sa, g2), op=ALU.add)
                        v.tensor_tensor(_sdiag(Sf32, g2, 1), _sdiag(Sf32, g2, 1),
                                        _psslice(PS_Sb, g2), op=ALU.add)
                        v.tensor_copy(_sdiag(Ssb, g2, 0), _sdiag(Sf32, g2, 0))
                        v.tensor_copy(_sdiag(Ssb, g2, 1), _sdiag(Sf32, g2, 1))
                    scp_mark[0] = p.val('ch_v')

        for tt in range(4):
            t = s * 4 + tt
            p.wait('pe', 'ch_v', oz_mark[0])
            for d in range(ND):
                i = pe.transpose(B[5][:, d * 64:(d + 1) * 64].bitcast(BF16),
                                 OT[:, tt * D + d * 128: tt * D + (d + 1) * 128],
                                 IDb[:])
            p.inc(i, 'tp')
            p.wait('v', 'tp')
            v.tensor_copy(OTT[:].rearrange("p (d c) -> p d c", c=128),
                          B[5][:, 0:512].bitcast(BF16).rearrange(
                              "p (d c) -> p d c", c=128))
            p.wait('pe', 'ch_v')
            for dh in range(2):
                bank = B[dh]
                for di in range(ND):
                    pe.matmul(bank[:],
                              OTT[:, di * 128:(di + 1) * 128],
                              W["o"][:, di * D + dh * 512: di * D + (dh + 1) * 512],
                              start=(di == 0), stop=False)
                i = pe.matmul(bank[:], ONESr[:],
                              BOb[:, dh * 512:(dh + 1) * 512],
                              start=False, stop=True)
                p.inc(i, 'mm')
            p.wait('v', 'mm')
            emit_ln_stats(p, [B[0][:], B[1][:]], BN, MV, RSTD, NMR)
            xt = X[:, t * D:(t + 1) * D]
            for dh in range(2):
                v.scalar_tensor_tensor(xt[:, dh * 512:(dh + 1) * 512],
                                       B[dh][:], RSTD[:],
                                       xt[:, dh * 512:(dh + 1) * 512],
                                       op0=ALU.mult, op1=ALU.add)
            v.tensor_scalar(xt, xt, NMR[:], None, op0=ALU.add)
            p.wait('pe', 'ch_v')


def _po(B):
    return B[4]


def _fm_head(QF, h, c4):
    d = h // 2
    part = (h % 2) * 64
    return QF[part:part + 64, d * 512 + c4 * 128: d * 512 + (c4 + 1) * 128]


def _sdiag(S, g2, parity):
    part = parity * 64
    return S[part:part + 64, g2 * 128 + part: g2 * 128 + part + 64]


def _psslice(PS, g2):
    part = 0 if PS.name.endswith("Sa") else 64
    return PS[part:part + 64, g2 * 64:(g2 + 1) * 64]


def _po_vals(po):
    return po[:, 0:256].rearrange("p (k f) -> p k f", f=64)


def _ssb_wave(Ssb, w):
    g0 = w * 2
    return Ssb[:, g0 * 64:(g0 + 2) * 64]


def _ps_wave(PS, w):
    g0 = (w % 2) * 2
    return PS[:, g0 * 64:(g0 + 2) * 64]


def build_l0_attn(detect_races=True, upto=99):
    nc = bass.Bass("TRN2", target_bir_lowering=False,
                   detect_race_conditions=detect_races)
    ap = {}
    ap['x'] = nc.dram_tensor("x", [T, D], F32, kind="ExternalInput").ap()
    for n in ("Wq", "Wk", "Wv", "Wo"):
        ap[n] = nc.dram_tensor(n, [D, D], F32, kind="ExternalInput").ap()
    ap['bo'] = nc.dram_tensor("bo", [1, D], F32, kind="ExternalInput").ap()
    for n, arr in host_consts().items():
        ap[n] = nc.dram_tensor(n, list(arr.shape), F32, kind="ExternalInput").ap()
    xo = nc.dram_tensor("xo", [T, D], F32, kind="ExternalOutput").ap()

    with ExitStack() as st:
        def sb(name, shape, dt):
            return st.enter_context(nc.sbuf_tensor(name, shape, dt))

        def ps(name, shape, dt):
            return st.enter_context(nc.psum_tensor(name, shape, dt))

        sb_ = alloc_p1(sb, ps)

        def script(p):
            emit_l0_attn(p, ap, sb_, upto=upto)
            emit_store_X(p, sb_, xo)

        run_script(nc, script, SEM_NAMES)
    return nc


def alloc_p1(sb, ps):
    sb_ = {}
    sb_['X'] = sb("X", [128, NT * D], F32)
    sb_['W'] = {n: sb("sW" + n, [128, ND * D], BF16) for n in ("q", "k", "v", "o")}
    sb_['LNb'] = sb("LNb", [128, D], BF16)
    sb_['LT'] = sb("LT", [128, ND * 512], BF16)
    sb_['QF'] = sb("QF", [128, ND * 512], BF16)
    sb_['KF'] = sb("KF", [128, ND * 512], BF16)
    sb_['KT'] = sb("KT", [128, 4 * D], BF16)
    sb_['VT'] = sb("VT", [128, 4 * D], BF16)
    sb_['KCT'] = sb("KCT", [128, 4 * D], BF16)
    sb_['OT'] = sb("OT", [128, 4 * D], BF16)
    sb_['OTT'] = sb("OTT", [128, D], BF16)
    sb_['Ssb'] = sb("Ssb", [128, 8 * 128], BF16)
    sb_['Sf32'] = sb("Sf32", [128, 8 * 128], F32)
    sb_['ATs'] = sb("ATs", [128, 2 * 512], BF16)
    sb_['T1'] = sb("T1", [128, 512], F32)
    sb_['T2'] = sb("T2", [128, 512], F32)
    sb_['BN'] = sb("BN", [128, 12], F32)
    sb_['MV'] = sb("MV", [128, 2], F32)
    sb_['RSTD'] = sb("RSTD", [128, 1], F32)
    sb_['NMR'] = sb("NMR", [128, 1], F32)
    sb_['ZRall'] = sb("ZRall", [128, 64], F32)
    sb_['IDb'] = sb("IDb", [128, 128], BF16)
    sb_['MT4'] = sb("MT4", [128, 512], BF16)
    sb_['PCUM'] = sb("PCUM", [128, 256], BF16)
    sb_['ONESr'] = sb("ONESr", [1, 128], BF16)
    sb_['ZROW'] = sb("ZROW", [1, 512], BF16)
    sb_['BOb'] = sb("BOb", [1, D], BF16)
    sb_['B'] = [ps(f"bank{i}", [128, 512], F32) for i in range(6)]
    sb_['PS_Sa'] = ps("psSa", [128, 512], F32)
    sb_['PS_Sb'] = ps("psSb", [128, 512], F32)
    return sb_


def emit_store_X(p, sb_, xo):
    for s_ in ('dy', 'dma', 'ch_v', 'ch_a', 'mm', 'tp'):
        p.wait('g', s_, p.val(s_))
    xor = xo.rearrange("(t p) d -> t p d", p=128)
    for t in range(NT):
        i = p.g.dma_start(out=xor[t],
                          in_=sb_['X'][:, t * D:(t + 1) * D])
        p.inc(i, 'xd', 16)
    p.wait('g', 'xd')



def emit_l0_ffn(p, ap, sb_):
    import os
    pe, v, a, g, y = p.pe, p.v, p.a, p.g, p.y
    LNb = sb_['LNb']; LTF = sb_['LTF']; HB = sb_['HB']; OUT = sb_['OUT']
    XT = sb_['XT']
    W1c = sb_['W1c']; W2c = sb_['W2c']
    B1b = sb_['B1b']; B2b = sb_['B2b']; WRb = sb_['WRb']; BRb = sb_['BRb']
    GL = sb_['GL']
    XS = sb_['HB'][:, 0:256].bitcast(F32)
    U1 = sb_['HB'][:, 256:512].bitcast(F32)
    U2 = sb_['HB'][:, 512:768].bitcast(F32)
    TAU = sb_['TAU']; TAUM = sb_['TAUM']; FLO = sb_['FLO']; FM = sb_['FM']
    CND = sb_['CND']; MAXV = sb_['MAXV']
    BN = sb_['BN']; MV = sb_['MV']; RSTD = sb_['RSTD']; NMR = sb_['NMR']
    IDb = sb_['IDb']; ONESr = sb_['ONESr']; ONES512 = sb_['ONES512']
    B = sb_['B']

    DM0 = 1.0 - (1.0 / 8.0) ** 0.5

    i = g.dma_start(out=IDb[:], in_=ap['c_id']); p.inc(i, 'dcs', 16)
    i = g.dma_start(out=ONESr[:], in_=ap['c_ones_row']); p.inc(i, 'dcs', 16)
    i = g.dma_start(out=ONES512[:], in_=ap['c_ones512']); p.inc(i, 'dcs', 16)
    i = g.dma_start(out=WRb[:].rearrange("p (a e) -> p a e", e=8),
                    in_=ap['Wr'].rearrange("(a p) e -> p a e", p=128))
    p.inc(i, 'dcs', 16)
    i = g.dma_start(out=BRb[:], in_=ap['br']); p.inc(i, 'dcs', 16)
    consts_loaded = p.val('dcs')

    xr = ap['x'].rearrange("(t p) d -> t p d", p=128)
    for t in range(NT):
        xsem = 'dy' if t % 2 == 0 else 'dz'
        if t > 0:
            p.wait('y', 'ch_v', xt_free[0])
        i = y.dma_start(out=XT[:, 0:D], in_=xr[t])
        p.inc(i, xsem, 16)
        p.wait('v', xsem, p.val(xsem))
        xt = XT[:, 0:D]
        emit_ln_stats(p, [xt[:, 0:512], xt[:, 512:1024]], BN, MV, RSTD, NMR)
        v.tensor_scalar(LNb[:], xt, RSTD[:], NMR[:], op0=ALU.mult, op1=ALU.add)
        xt_free = [p.val('ch_v')]
        p.wait('pe', 'ch_v')
        if t == 0:
            p.wait('pe', 'dcs', consts_loaded)
        for d in range(ND):
            i = pe.transpose(B[5][:, d * 64:(d + 1) * 64].bitcast(BF16),
                             LNb[:, d * 128:(d + 1) * 128], IDb[:])
        p.inc(i, 'tp')
        p.wait('v', 'tp')
        v.tensor_copy(
            LTF[:].rearrange("p (d c) -> p d c", c=T)[:, :, t * 128:(t + 1) * 128],
            B[5][:, 0:512].bitcast(BF16).rearrange("p (d c) -> p d c", c=128))
        p.wait('pe', 'ch_v')
        for di in range(ND):
            pe.matmul(B[4][:, 0:8],
                      LTF[:, di * T + t * 128: di * T + (t + 1) * 128],
                      WRb[:, di * 8:(di + 1) * 8],
                      start=(di == 0), stop=False)
        i = pe.matmul(B[4][:, 0:8], ONESr[:], BRb[:], start=False, stop=True)
        p.inc(i, 'mm')
        p.wait('v', 'mm')
        v.tensor_copy(GL[:, t * 8:(t + 1) * 8], B[4][:, 0:8])
        p.wait('pe', 'ch_v')

    g3 = GL[:].rearrange("p (g e) -> p g e", e=8)
    v.tensor_scalar(XS[:], GL[:], 0.5, None, op0=ALU.mult)
    x3 = XS[:].rearrange("p (g e) -> p g e", e=8)
    v.tensor_reduce(MAXV[:].unsqueeze(2), x3, axis=AX.X, op=ALU.max)
    v.tensor_scalar(TAU[:], MAXV[:], 1.0, None, op0=ALU.subtract)
    v.tensor_tensor(U1[:].rearrange("p (g e) -> p g e", e=8), x3,
                    TAU[:].unsqueeze(2).broadcast_to([128, 16, 8]), op=ALU.subtract)
    v.tensor_scalar(U1[:], U1[:], 0.0, None, op0=ALU.max)
    v.tensor_tensor(U2[:], U1[:], U1[:], op=ALU.mult)
    v.tensor_reduce(FLO[:].unsqueeze(2), U2[:].rearrange("p (g e) -> p g e", e=8),
                    axis=AX.X, op=ALU.add)
    v.tensor_scalar(FLO[:], FLO[:], 1.0, None, op0=ALU.subtract)
    dm = DM0
    for it in range(25):
        dm = dm * 0.5
        v.tensor_scalar(TAUM[:], TAU[:], dm, None, op0=ALU.add)
        v.tensor_tensor(U1[:].rearrange("p (g e) -> p g e", e=8), x3,
                        TAUM[:].unsqueeze(2).broadcast_to([128, 16, 8]),
                        op=ALU.subtract)
        v.tensor_scalar(U1[:], U1[:], 0.0, None, op0=ALU.max)
        v.tensor_tensor(U2[:], U1[:], U1[:], op=ALU.mult)
        v.tensor_reduce(FM[:].unsqueeze(2), U2[:].rearrange("p (g e) -> p g e", e=8),
                        axis=AX.X, op=ALU.add)
        v.tensor_scalar(FM[:], FM[:], 1.0, None, op0=ALU.subtract)
        v.tensor_tensor(CND[:], FM[:], FLO[:], op=ALU.mult)
        p.wait('a', 'ch_v')
        a.activation(out=CND[:], in_=CND[:], func=AF.Sign)
        p.wait('v', 'ch_a')
        v.tensor_scalar(CND[:], CND[:], 1.0, 1.0, op0=ALU.add, op1=ALU.min)
        v.scalar_tensor_tensor(TAU[:], CND[:], dm, TAU[:], op0=ALU.mult, op1=ALU.add)
    v.tensor_tensor(U1[:].rearrange("p (g e) -> p g e", e=8), x3,
                    TAUM[:].unsqueeze(2).broadcast_to([128, 16, 8]), op=ALU.subtract)
    v.tensor_scalar(U1[:], U1[:], 0.0, None, op0=ALU.max)
    v.tensor_tensor(U2[:], U1[:], U1[:], op=ALU.mult)
    v.tensor_reduce(FM[:].unsqueeze(2), U2[:].rearrange("p (g e) -> p g e", e=8),
                    axis=AX.X, op=ALU.add)
    v.reciprocal(FM[:], FM[:])
    v.tensor_tensor(GL[:].rearrange("p (g e) -> p g e", e=8),
                    U2[:].rearrange("p (g e) -> p g e", e=8),
                    FM[:].unsqueeze(2).broadcast_to([128, 16, 8]), op=ALU.mult)
    v.tensor_scalar(CND2 := U1, GL[:], 1e-9, None, op0=ALU.subtract)
    p.wait('a', 'ch_v')
    a.activation(out=U1[:], in_=U1[:], func=AF.Sign)
    p.wait('v', 'ch_a')
    v.tensor_scalar(U1[:], U1[:], 1.0, 1.0, op0=ALU.add, op1=ALU.min)
    v.tensor_tensor(GL[:], GL[:], U1[:], op=ALU.mult)
    for t in range(NT):
        v.memset(OUT[:, t * D:(t + 1) * D], 0.0)
    gates_done_v = p.val('ch_v')

    if os.environ.get("L0F_SKIP"):
        p.wait('g', 'ch_v', p.val('ch_v'))
        for t in range(NT):
            i = g.dma_start(out=ap['xo'].rearrange("(t p) d -> t p d", p=128)[t],
                            in_=OUT[:, t * D:(t + 1) * D])
            p.inc(i, 'xd', 16)
        p.wait('g', 'xd')
        return
    NFC = 4
    mm_after_chunk = []

    def stream_chunk(e, fc, buf):
        wsem = 'wda' if buf == 0 else 'wdb'
        w1r = ap['W1']
        w2r = ap['W2']
        for di in range(ND):
            i = g.dma_start(out=W1c[buf][:, di * 1024:(di + 1) * 1024],
                            in_=w1r[e, di * 128:(di + 1) * 128,
                                    fc * 1024:(fc + 1) * 1024])
            p.inc(i, wsem, 16)
        for ft in range(8):
            i = g.dma_start(out=W2c[buf][:, ft * 1024:(ft + 1) * 1024],
                            in_=w2r[e, fc * 1024 + ft * 128: fc * 1024 + (ft + 1) * 128, :])
            p.inc(i, wsem, 16)
        i = g.dma_start(out=B1b[buf][:],
                        in_=ap['b1'][e:e + 1, fc * 1024:(fc + 1) * 1024])
        p.inc(i, wsem, 16)
        if fc == 0:
            i = g.dma_start(out=B2b[buf][:], in_=ap['b2'][e:e + 1, :])
            p.inc(i, wsem, 16)

    stream_chunk(0, 0, 0)
    dma_after_chunk = [('wda', p.val('wda'))]
    NCHUNK_TOTAL = int(os.environ.get("L0F_NE", "8")) * NFC

    NE = int(os.environ.get("L0F_NE", "8"))
    ci = 0
    for e in range(NE):
        for fc in range(NFC):
            buf = ci % 2
            nci = ci + 1
            if nci < NCHUNK_TOTAL:
                ne, nfc = nci // NFC, nci % NFC
                if len(mm_after_chunk) >= 1:
                    p.wait('g', 'mm', mm_after_chunk[-1])
                stream_chunk(ne, nfc, nci % 2)
                dma_after_chunk.append(('wda' if nci % 2 == 0 else 'wdb',
                                        p.val('wda' if nci % 2 == 0 else 'wdb')))
            p.wait('pe', dma_after_chunk[ci][0], dma_after_chunk[ci][1])
            if ci == 0:
                p.wait('pe', 'ch_v', gates_done_v)
            else:
                p.wait('pe', 'ch_v', p.val('ch_v'))
                p.wait('pe', 'ch_a', p.val('ch_a'))

            ha_marks = {}
            for ft in range(8):
                hb = 0 if ft % 2 == 0 else 4
                if ft >= 2:
                    p.wait('pe', 'ch_a', ha_marks[ft - 2])
                for di in range(ND):
                    for tm in range(4):
                        pe.matmul(B[hb + tm][:],
                                  W1c[buf][:, di * 1024 + ft * 128: di * 1024 + (ft + 1) * 128],
                                  LTF[:, di * T + tm * 512: di * T + (tm + 1) * 512],
                                  start=(di == 0), stop=False)
                for tm in range(4):
                    i = pe.matmul(B[hb + tm][:],
                                  B1b[buf][0:1, ft * 128:(ft + 1) * 128],
                                  ONES512[:], start=False, stop=True)
                p.inc(i, 'mm')
                p.wait('a', 'mm')
                for tm in range(4):
                    a.activation(out=HB[:, ft * T + tm * 512: ft * T + (tm + 1) * 512],
                                 in_=B[hb + tm][:], func=AF.Relu)
                ha_marks[ft] = p.val('ch_a')
            p.wait('pe', 'ch_a', ha_marks[7])

            yv_marks = {}
            for tt in range(NT):
                yb = 4 if tt % 2 == 0 else 6
                if tt >= 2:
                    p.wait('pe', 'ch_v', yv_marks[tt - 2])
                for ft in range(8):
                    for dh in range(2):
                        i = pe.matmul(B[yb + dh][:],
                                      HB[:, ft * T + tt * 128: ft * T + (tt + 1) * 128],
                                      W2c[buf][:, ft * 1024 + dh * 512: ft * 1024 + (dh + 1) * 512],
                                      start=(ft == 0),
                                      stop=(fc != 0 and ft == 7))
                if fc == 0:
                    for dh in range(2):
                        i = pe.matmul(B[yb + dh][:], ONESr[:],
                                      B2b[buf][0:1, dh * 512:(dh + 1) * 512],
                                      start=False, stop=True)
                p.inc(i, 'mm')
                p.wait('v', 'mm')
                gcol = GL[:, tt * 8 + e: tt * 8 + e + 1]
                for dh in range(2):
                    v.scalar_tensor_tensor(
                        OUT[:, tt * D + dh * 512: tt * D + (dh + 1) * 512],
                        B[yb + dh][:], gcol,
                        OUT[:, tt * D + dh * 512: tt * D + (dh + 1) * 512],
                        op0=ALU.mult, op1=ALU.add)
                yv_marks[tt] = p.val('ch_v')
            mm_after_chunk.append(p.val('mm'))
            ci += 1

    for t in range(NT):
        xsem = 'dy' if t % 2 == 0 else 'dz'
        if t > 0:
            p.wait('y', 'ch_v', p.val('ch_v'))
        else:
            p.wait('y', 'ch_v', xt_free[0])
        i = y.dma_start(out=XT[:, 0:D], in_=xr[t])
        p.inc(i, xsem, 16)
        ot = OUT[:, t * D:(t + 1) * D]
        emit_ln_stats(p, [ot[:, 0:512], ot[:, 512:1024]], BN, MV, RSTD, NMR)
        p.wait('v', xsem, p.val(xsem))
        xt = XT[:, 0:D]
        v.scalar_tensor_tensor(ot, ot, RSTD[:], NMR[:].broadcast_to([128, 1]) if False
                               else ot, op0=ALU.mult, op1=ALU.bypass) if False else None
        v.tensor_scalar(ot, ot, RSTD[:], NMR[:], op0=ALU.mult, op1=ALU.add)
        v.tensor_tensor(ot, ot, xt, op=ALU.add)
        fin = p.val('ch_v')
        p.wait('g', 'ch_v', fin)
        i = g.dma_start(out=ap['xo'].rearrange("(t p) d -> t p d", p=128)[t], in_=ot)
        p.inc(i, 'xd', 16)
    p.wait('g', 'xd')


def build_l0_ffn(detect_races=True):
    nc = bass.Bass("TRN2", target_bir_lowering=False,
                   detect_race_conditions=detect_races)
    ap = {}
    ap['x'] = nc.dram_tensor("x", [T, D], F32, kind="ExternalInput").ap()
    ap['Wr'] = nc.dram_tensor("Wr", [D, 8], F32, kind="ExternalInput").ap()
    ap['br'] = nc.dram_tensor("br", [1, 8], F32, kind="ExternalInput").ap()
    ap['W1'] = nc.dram_tensor("W1", [8, D, 4096], F32, kind="ExternalInput").ap()
    ap['b1'] = nc.dram_tensor("b1", [8, 4096], F32, kind="ExternalInput").ap()
    ap['W2'] = nc.dram_tensor("W2", [8, 4096, D], F32, kind="ExternalInput").ap()
    ap['b2'] = nc.dram_tensor("b2", [8, D], F32, kind="ExternalInput").ap()
    for n, arr in host_consts().items():
        ap[n] = nc.dram_tensor(n, list(arr.shape), F32, kind="ExternalInput").ap()
    ap['xo'] = nc.dram_tensor("xo", [T, D], F32, kind="ExternalOutput").ap()

    with ExitStack() as st:
        def sb(name, shape, dt):
            return st.enter_context(nc.sbuf_tensor(name, shape, dt))

        def ps(name, shape, dt):
            return st.enter_context(nc.psum_tensor(name, shape, dt))

        sb_ = alloc_p2(sb, ps)

        def script(p):
            emit_l0_ffn(p, ap, sb_)

        run_script(nc, script, SEM_NAMES)
    return nc


def alloc_p2(sb, ps):
    sb_ = {}
    sb_['XT'] = sb("XT", [128, D], F32)
    sb_['LNb'] = sb("LNb", [128, D], BF16)
    sb_['LTF'] = sb("LTF", [128, ND * T], BF16)
    sb_['HB'] = sb("HB", [128, 8 * T], BF16)
    sb_['OUT'] = sb("OUT", [128, NT * D], F32)
    sb_['W1c'] = [sb(f"W1c{i}", [128, 8 * 1024], BF16) for i in range(2)]
    sb_['W2c'] = [sb(f"W2c{i}", [128, 8 * 1024], BF16) for i in range(2)]
    sb_['B1b'] = [sb(f"B1b{i}", [1, 1024], BF16) for i in range(2)]
    _b2 = sb("B2b0", [1, 1024], BF16)
    sb_['B2b'] = [_b2, _b2]
    sb_['WRb'] = sb("WRb", [128, ND * 8], BF16)
    sb_['BRb'] = sb("BRb", [1, 8], BF16)
    sb_['GL'] = sb("GL", [128, 128], F32)
    sb_['XS'] = None
    sb_['U1'] = None
    sb_['U2'] = None
    sb_['TAU'] = sb("TAU", [128, 16], F32)
    sb_['TAUM'] = sb("TAUM", [128, 16], F32)
    sb_['FLO'] = sb("FLO", [128, 16], F32)
    sb_['FM'] = sb("FM", [128, 16], F32)
    sb_['CND'] = sb("CND", [128, 16], F32)
    sb_['MAXV'] = sb("MAXV", [128, 16], F32)
    sb_['BN'] = sb("BN", [128, 12], F32)
    sb_['MV'] = sb("MV", [128, 2], F32)
    sb_['RSTD'] = sb("RSTD", [128, 1], F32)
    sb_['NMR'] = sb("NMR", [128, 1], F32)
    sb_['IDb'] = sb("IDb", [128, 128], BF16)
    sb_['ONESr'] = sb("ONESr", [1, 128], BF16)
    sb_['ONES512'] = sb("ONES512", [1, 512], BF16)
    sb_['B'] = [ps(f"bank{i}", [128, 512], F32) for i in range(8)]
    return sb_



def emit_l1_attn(p, ap, sb_):
    pe, v, a, g, y = p.pe, p.v, p.a, p.g, p.y
    XT = sb_['XT']; LNb = sb_['LNb']; LTF = sb_['LTF']; QF = sb_['QF']
    Wq = sb_['Wq']; Wkv = sb_['Wkv']; Wo = sb_['Wo']
    KST = sb_['KST']; VST = sb_['VST']; PRD = sb_['PRD']
    EE = sb_['EE']; NM = sb_['NM']; ZR = sb_['ZR']; ACC = sb_['ACC']
    OTT = sb_['OTT']; KTMP = sb_['KTMP']; VTMP = sb_['VTMP']
    BN = sb_['BN']; MV = sb_['MV']; RSTD = sb_['RSTD']; NMR = sb_['NMR']
    IDb = sb_['IDb']; BLK = sb_['BLK']
    B = sb_['B']; SC = sb_['SC']
    G1i, G1o, G2i, G2o = sb_['G1i'], sb_['G1o'], sb_['G2i'], sb_['G2o']

    i = g.dma_start(out=IDb[:], in_=ap['c_id']); p.inc(i, 'dcs', 16)
    i = g.dma_start(out=BLK[:], in_=ap['c_blkdt']); p.inc(i, 'dcs', 16)
    for di in range(ND):
        i = g.dma_start(out=Wq[:, di * D:(di + 1) * D],
                        in_=ap['Wq'].rearrange("(a p) d -> a p d", p=128)[di])
        p.inc(i, 'dcs', 16)
        i = g.dma_start(out=Wkv[:, di * 2048:(di + 1) * 2048],
                        in_=ap['Wkv'].rearrange("(a p) d -> a p d", p=128)[di])
        p.inc(i, 'dcs', 16)
        i = g.dma_start(out=Wo[:, di * D:(di + 1) * D],
                        in_=ap['Wo'].rearrange("(a p) d -> a p d", p=128)[di])
        p.inc(i, 'dcs', 16)
    consts_loaded = p.val('dcs')

    xr = ap['x'].rearrange("(t p) d -> t p d", p=128)
    for t in range(NT):
        xsem = 'dy' if t % 2 == 0 else 'dz'
        if t > 0:
            p.wait('y', 'ch_v', xt_free[0])
        i = y.dma_start(out=XT[:, 0:D], in_=xr[t])
        p.inc(i, xsem, 16)
        p.wait('v', xsem, p.val(xsem))
        xt = XT[:, 0:D]
        emit_ln_stats(p, [xt[:, 0:512], xt[:, 512:1024]], BN, MV, RSTD, NMR)
        v.tensor_scalar(LNb[:], xt, RSTD[:], NMR[:], op0=ALU.mult, op1=ALU.add)
        xt_free = [p.val('ch_v')]
        p.wait('pe', 'ch_v')
        if t == 0:
            p.wait('pe', 'dcs', consts_loaded)
        for d in range(ND):
            i = pe.transpose(B[3][:, d * 64:(d + 1) * 64].bitcast(BF16),
                             LNb[:, d * 128:(d + 1) * 128], IDb[:])
        p.inc(i, 'tp')
        p.wait('v', 'tp')
        v.tensor_copy(
            LTF[:].rearrange("p (d c) -> p d c", c=T)[:, :, t * 128:(t + 1) * 128],
            B[3][:, 0:512].bitcast(BF16).rearrange("p (d c) -> p d c", c=128))
        p.wait('pe', 'ch_v')

    for dt in range(8):
        p.wait('v', 'gd', p.val('gd'))
        for tc in range(4):
            bank = B[2] if tc % 2 == 0 else B[3]
            for di in range(ND):
                i = pe.matmul(bank[:],
                              Wkv[:, di * 2048 + dt * 128: di * 2048 + (dt + 1) * 128],
                              LTF[:, di * T + tc * 512: di * T + (tc + 1) * 512],
                              start=(di == 0), stop=(di == ND - 1))
            p.inc(i, 'mm')
            p.wait('v', 'mm')
            v.tensor_copy(KTMP[:, tc * 512:(tc + 1) * 512], bank[:])
            p.wait('pe', 'ch_v')
        kd = p.val('ch_v')
        p.wait('g', 'ch_v', kd)
        i = g.dma_start(out=G1i[dt * 128:(dt + 1) * 128, :], in_=KTMP[:])
        p.inc(i, 'gd', 16)

    for t in range(NT):
        p.wait('v', 'gd', p.val('gd'))
        for dh in range(2):
            bank = B[2] if dh == 0 else B[3]
            for di in range(ND):
                i = pe.matmul(bank[:],
                              LTF[:, di * T + t * 128: di * T + (t + 1) * 128],
                              Wkv[:, di * 2048 + 1024 + dh * 512:
                                  di * 2048 + 1024 + (dh + 1) * 512],
                              start=(di == 0), stop=(di == ND - 1))
            p.inc(i, 'mm')
        p.wait('v', 'mm')
        v.tensor_copy(VTMP[:, 0:512], B[2][:])
        v.tensor_copy(VTMP[:, 512:1024], B[3][:])
        vd = p.val('ch_v')
        p.wait('pe', 'ch_v')
        p.wait('g', 'ch_v', vd)
        i = g.dma_start(out=G2i[t * 128:(t + 1) * 128, :], in_=VTMP[:])
        p.inc(i, 'gd', 16)

    for dt in range(8):
        for tc in range(4):
            bank = B[2] if tc % 2 == 0 else B[3]
            for di in range(ND):
                i = pe.matmul(bank[:],
                              Wq[:, di * D + dt * 128: di * D + (dt + 1) * 128],
                              LTF[:, di * T + tc * 512: di * T + (tc + 1) * 512],
                              start=(di == 0), stop=(di == ND - 1))
            p.inc(i, 'mm')
            p.wait('v', 'mm')
            v.tensor_copy(QF[:, dt * T + tc * 512: dt * T + (tc + 1) * 512], bank[:])
            p.wait('pe', 'ch_v')

    p.wait('g', 'gd', p.val('gd'))
    i = g.collective_compute("AllGather", ALU.bypass,
                             replica_groups=[list(range(NCORES))],
                             ins=[G1i.opt()], outs=[G1o.opt()])
    p.inc(i, 'cc', 1)
    i = g.collective_compute("AllGather", ALU.bypass,
                             replica_groups=[list(range(NCORES))],
                             ins=[G2i.opt()], outs=[G2o.opt()])
    p.inc(i, 'cc', 1)
    cc_done = p.val('cc')

    p.wait('g', 'cc', cc_done)
    kst_free = {0: 0, 1: 0}
    for j in range(NCORES):
        ksem = 'kda' if j % 2 == 0 else 'kdb'
        buf = KST[:, (j % 2) * T: (j % 2) * T + T]
        for dt in range(8):
            p.wait('g', 'ch_v', kst_free[j % 2])
            i = g.dma_start(out=buf, in_=G1o[j * 1024 + dt * 128:
                                             j * 1024 + (dt + 1) * 128, :])
            p.inc(i, ksem, 16)
            p.wait('v', ksem, p.val(ksem))
            p.wait('v', 'scm', p.val('scm'))
            v.tensor_tensor(PRD[:], QF[:, dt * T:(dt + 1) * T], buf, op=ALU.mult)
            kst_free[j % 2] = p.val('ch_v')
            p.wait('pe', 'ch_v')
            for tt in range(NT):
                i = pe.matmul(SC[:, j * 256 + tt * 16: j * 256 + (tt + 1) * 16],
                              PRD[:, tt * 128:(tt + 1) * 128],
                              BLK[:, dt * 16:(dt + 1) * 16],
                              start=(j % 2 == 0 and tt == 0 and dt == 0),
                              stop=(j % 2 == 1 and tt == NT - 1 and dt == 7))
            p.inc(i, 'scm')

    p.wait('v', 'scm', p.val('scm'))
    scj = SC[:, 0:2048].rearrange("p (j c) -> p j c", j=8).transpose([0, 2, 1])
    v.tensor_reduce(NM[:].unsqueeze(2), scj, axis=AX.X, op=ALU.max, negate=True)
    v.tensor_tensor(EE[:].rearrange("p (j c) -> p j c", j=8),
                    SC[:, 0:2048].rearrange("p (j c) -> p j c", j=8),
                    NM[:].unsqueeze(1).broadcast_to([128, 8, 256]),
                    op=ALU.add)
    p.wait('a', 'ch_v')
    a.activation(out=EE[:], in_=EE[:], func=AF.Exp, scale=0.125)
    p.wait('v', 'ch_a')
    v.tensor_reduce(ZR[:].unsqueeze(2),
                    EE[:].rearrange("p (j c) -> p j c", j=8).transpose([0, 2, 1]),
                    axis=AX.X, op=ALU.add)
    v.reciprocal(ZR[:], ZR[:])
    v.tensor_tensor(EE[:].rearrange("p (j c) -> p j c", j=8),
                    EE[:].rearrange("p (j c) -> p j c", j=8),
                    ZR[:].unsqueeze(1).broadcast_to([128, 8, 256]), op=ALU.mult)

    vst_free = {0: 0, 1: 0}
    for t in range(NT):
        vsem = 'vda' if t % 2 == 0 else 'vdb'
        vbuf = VST[:, (t % 2) * 8192: (t % 2) * 8192 + 8192]
        p.wait('g', 'ch_v', vst_free[t % 2])
        for j in range(NCORES):
            i = g.dma_start(out=vbuf[:, j * 1024:(j + 1) * 1024],
                            in_=G2o[j * 2048 + t * 128: j * 2048 + (t + 1) * 128, :])
            p.inc(i, vsem, 16)
        p.wait('v', vsem, p.val(vsem))
        p.wait('v', 'xd', p.val('xd'))
        for j in range(NCORES):
            pj = EE[:, j * 256 + t * 16: j * 256 + (t + 1) * 16]
            if j == 0:
                v.tensor_tensor(ACC[:].rearrange("p (h f) -> p h f", h=16),
                                vbuf[:, 0:1024].rearrange("p (h f) -> p h f", h=16),
                                pj.unsqueeze(2).broadcast_to([128, 16, 64]),
                                op=ALU.mult)
            else:
                v.tensor_tensor(PRD[:, 0:1024].rearrange("p (h f) -> p h f", h=16),
                                vbuf[:, j * 1024:(j + 1) * 1024].rearrange(
                                    "p (h f) -> p h f", h=16),
                                pj.unsqueeze(2).broadcast_to([128, 16, 64]),
                                op=ALU.mult)
                v.tensor_tensor(ACC[:], ACC[:], PRD[:, 0:1024], op=ALU.add)
        v.tensor_copy(LNb[:], ACC[:])
        vst_free[t % 2] = p.val('ch_v')
        p.wait('pe', 'ch_v')
        for d in range(ND):
            i = pe.transpose(B[3][:, d * 64:(d + 1) * 64].bitcast(BF16),
                             LNb[:, d * 128:(d + 1) * 128], IDb[:])
        p.inc(i, 'tp')
        p.wait('v', 'tp')
        v.tensor_copy(OTT[:].rearrange("p (d c) -> p d c", c=128),
                      B[3][:, 0:512].bitcast(BF16).rearrange("p (d c) -> p d c", c=128))
        p.wait('pe', 'ch_v')
        for dh in range(2):
            bank = B[dh]
            for di in range(ND):
                i = pe.matmul(bank[:],
                              OTT[:, di * 128:(di + 1) * 128],
                              Wo[:, di * D + dh * 512: di * D + (dh + 1) * 512],
                              start=(di == 0), stop=(di == ND - 1))
            p.inc(i, 'mm')
        p.wait('v', 'mm')
        emit_ln_stats(p, [B[0][:], B[1][:]], BN, MV, RSTD, NMR)
        p.wait('y', 'ch_v', p.val('ch_v'))
        i = y.dma_start(out=XT[:, 0:D], in_=xr[t])
        p.inc(i, 'dxf', 16)
        p.wait('v', 'dxf', p.val('dxf'))
        for dh in range(2):
            v.scalar_tensor_tensor(ACC[:, dh * 512:(dh + 1) * 512],
                                   B[dh][:], RSTD[:],
                                   XT[:, dh * 512:(dh + 1) * 512],
                                   op0=ALU.mult, op1=ALU.add)
        v.tensor_scalar(ACC[:, 0:1024], ACC[:, 0:1024], NMR[:], None, op0=ALU.add)
        fin = p.val('ch_v')
        p.wait('g', 'ch_v', fin)
        i = g.dma_start(out=ap['xo'].rearrange("(t p) d -> t p d", p=128)[t],
                        in_=ACC[:, 0:1024])
        p.inc(i, 'xd', 16)
    p.wait('g', 'xd')


def build_l1_attn(detect_races=True):
    nc = bass.Bass("TRN2", target_bir_lowering=False,
                   detect_race_conditions=detect_races)
    ap = {}
    ap['x'] = nc.dram_tensor("x", [T, D], F32, kind="ExternalInput").ap()
    ap['Wq'] = nc.dram_tensor("Wq", [D, D], F32, kind="ExternalInput").ap()
    ap['Wkv'] = nc.dram_tensor("Wkv", [D, 2048], F32, kind="ExternalInput").ap()
    ap['Wo'] = nc.dram_tensor("Wo", [D, D], F32, kind="ExternalInput").ap()
    for n, arr in host_consts().items():
        ap[n] = nc.dram_tensor(n, list(arr.shape), F32, kind="ExternalInput").ap()
    ap['xo'] = nc.dram_tensor("xo", [T, D], F32, kind="ExternalOutput").ap()

    with ExitStack() as st:
        def sb(name, shape, dt):
            return st.enter_context(nc.sbuf_tensor(name, shape, dt))

        def ps(name, shape, dt):
            return st.enter_context(nc.psum_tensor(name, shape, dt))

        gl = declare_p3_dram(nc)
        sb_ = alloc_p3(sb, ps, gl)

        run_script(nc, lambda p: emit_l1_attn(p, ap, sb_), SEM_NAMES)
    return nc


def declare_p3_dram(nc):
    return {
        'G1i': nc.dram_tensor("G1i", [1024, 2048], BF16).ap(),
        'G1o': nc.dram_tensor("G1o", [8 * 1024, 2048], BF16,
                              addr_space="Shared").ap(),
        'G2i': nc.dram_tensor("G2i", [2048, 1024], BF16).ap(),
        'G2o': nc.dram_tensor("G2o", [8 * 2048, 1024], BF16,
                              addr_space="Shared").ap(),
    }


def alloc_p3(sb, ps, gl):
    sb_ = {}
    sb_.update(gl)
    sb_['XT'] = sb("XT", [128, D], F32)
    sb_['LNb'] = sb("LNb", [128, D], BF16)
    sb_['LTF'] = sb("LTF", [128, ND * T], BF16)
    sb_['QF'] = sb("QF", [128, ND * T], BF16)
    sb_['Wq'] = sb("sWq", [128, ND * D], BF16)
    sb_['Wkv'] = sb("sWkv", [128, ND * 2048], BF16)
    sb_['Wo'] = sb("sWo", [128, ND * D], BF16)
    sb_['KST'] = sb("KST", [128, 2 * T], BF16)
    sb_['VST'] = sb("VST", [128, 2 * 8192], BF16)
    sb_['PRD'] = sb("PRD", [128, T], BF16)
    sb_['EE'] = sb("EE", [128, 2048], F32)
    sb_['NM'] = sb("NM", [128, 256], F32)
    sb_['ZR'] = sb("ZR", [128, 256], F32)
    sb_['ACC'] = sb("ACC", [128, 1024], F32)
    sb_['OTT'] = sb("OTT", [128, D], BF16)
    sb_['KTMP'] = sb("KTMP", [128, T], BF16)
    sb_['VTMP'] = sb("VTMP", [128, 1024], BF16)
    sb_['BN'] = sb("BN", [128, 12], F32)
    sb_['MV'] = sb("MV", [128, 2], F32)
    sb_['RSTD'] = sb("RSTD", [128, 1], F32)
    sb_['NMR'] = sb("NMR", [128, 1], F32)
    sb_['IDb'] = sb("IDb", [128, 128], BF16)
    sb_['BLK'] = sb("BLK", [128, 128], BF16)
    sb_['SC'] = ps("SC", [128, 2048], F32)
    sb_['B'] = [ps(f"bank{i}", [128, 512], F32) for i in range(4)]
    return sb_



def emit_l1_ffn(p, ap, sb_):
    pe, v, a, g, y = p.pe, p.v, p.a, p.g, p.y
    XI = sb_['XI']; AR = sb_['AR']; XT = sb_['XT']
    WQKV = sb_['WQKV']; WOL = sb_['WOL']
    W1A = sb_['W1A']; W1G = sb_['W1G']; W2T6 = sb_['W2T6']
    B1Ta = sb_['B1Ta']; B1Tg = sb_['B1Tg']; B2L = sb_['B2L']
    AQc = sb_['AQc']; BQc = sb_['BQc']; AKc = sb_['AKc']; BKc = sb_['BKc']
    PERMb = sb_['PERMb']; SHIFTP = sb_['SHIFTP']; CORNER = sb_['CORNER']
    AMTRIL = sb_['AMTRIL']; AMTRIU = sb_['AMTRIU']
    BLK2f = sb_['BLK2f']; PAT2f = sb_['PAT2f']; ONESF = sb_['ONESF']
    ONESr = sb_['ONESr']; ONE512b = sb_['ONE512b']; IDb = sb_['IDb']
    IDf = sb_['IDf']
    QS8 = sb_['QS8']; KSC = sb_['KSC']; QSL = sb_['QSL']
    LGC = sb_['LGC']; LBC = sb_['LBC']
    T1f = sb_['T1f']; QRAW = sb_['QRAW']
    XIB = sb_['XIB']; T2b = XIB[:, 0:512]; T3b = XIB[:, 512:1024]
    NS = sb_['NS']; PBUF = sb_['PBUF']; PTB = sb_['PTB']
    OTM = sb_['OTM']; OTT = sb_['OTT']; STS = sb_['STS']
    MUC = sb_['MUC']; TRS = sb_['TRS']; LASTC = sb_['LASTC']
    MUCH = sb_['MUCH']; RSCH = sb_['RSCH']
    ZC = sb_['ZC']; GTB = sb_['GTB']; GTB2 = sb_['GTB2']
    BN = sb_['BN']; MV = sb_['MV']; RSTD = sb_['RSTD']; NMR = sb_['NMR']
    B = sb_['B']
    MRD = sb_['MRD']

    AQFL = AR[:, 0:8192]
    AKFL = AR[:, 8192:16384]
    AXTF = AR[:, 16384:20512]
    AVTL = AR[:, 20512:28704]
    AHLN = AR[:, 0:16384]
    AUB = AR[:, 16384:28672]

    for nm, dst in (("c_id", IDb), ("c_perm", PERMb), ("c_shiftp", SHIFTP),
                    ("c_corner", CORNER), ("c_amask_tril", AMTRIL),
                    ("c_amask_triu", AMTRIU), ("c_blk2", BLK2f),
                    ("c_pat2", PAT2f), ("c_onesf", ONESF),
                    ("c_ones_row", ONESr), ("c_ones512", ONE512b),
                    ("c_idf", IDf)):
        i = g.dma_start(out=dst[:], in_=ap[nm]); p.inc(i, 'dcs', 16)
    consts_loaded = p.val('dcs')
    p.wait('pe', 'dcs', consts_loaded)
    p.wait('v', 'dcs', consts_loaded)
    p.wait('a', 'dcs', consts_loaded)

    xr = ap['x'].rearrange("(t p) d -> t p d", p=128)
    for t in range(NT):
        xsem = 'dy' if t % 2 == 0 else 'dz'
        if t > 0:
            p.wait('y', 'ch_v', xt_free[0])
        i = y.dma_start(out=XT[:, 0:D], in_=xr[t])
        p.inc(i, xsem, 16)
        p.wait('v', xsem, p.val(xsem))
        xt = XT[:, 0:D]
        emit_ln_stats(p, [xt[:, 0:512], xt[:, 512:1024]], BN, MV, RSTD, NMR)
        v.tensor_scalar(XI[:, t * D:(t + 1) * D], xt, RSTD[:], NMR[:],
                        op0=ALU.mult, op1=ALU.add)
        xt_free = [p.val('ch_v')]

    def build_xtf(tc):
        for tt in range(4):
            t = tc * 4 + tt
            v.tensor_copy(XIB[:], XI[:, t * D:(t + 1) * D])
            p.wait('pe', 'ch_v')
            for d in range(ND):
                i = pe.transpose(B[5][:, d * 64:(d + 1) * 64].bitcast(BF16),
                                 XIB[:, d * 128:(d + 1) * 128], IDb[:])
            p.inc(i, 'tp')
            p.wait('v', 'tp')
            v.tensor_copy(
                AXTF.rearrange("p (d c) -> p d c", c=516)[:, :, 1 + tt * 128:
                                                         1 + (tt + 1) * 128],
                B[5][:, 0:512].bitcast(BF16).rearrange("p (d c) -> p d c", c=128))
            p.wait('pe', 'ch_v')
        if tc == 0:
            v.memset(AXTF.rearrange("p (d c) -> p d c", c=516)[:, :, 0:1], 0.0)
        else:
            v.tensor_copy(AXTF.rearrange("p (d c) -> p d c", c=516)[:, :, 0:1],
                          LASTC[:].unsqueeze(2))
        v.tensor_copy(LASTC[:].unsqueeze(2),
                      AXTF.rearrange("p (d c) -> p d c", c=516)[:, :, 512:513])
        p.wait('pe', 'ch_v')

    def hslice(di, lo, n):
        off = 1 if di < 4 else 0
        return AXTF[:, di * 516 + off + lo: di * 516 + off + lo + n]

    for l in range(2):
        p.wait('g', 'mm', p.val('mm'))
        for di in range(ND):
            i = g.dma_start(out=WQKV[:, di * 1536:(di + 1) * 1536],
                            in_=ap['qkv_l%d' % l][di * 128:(di + 1) * 128, :])
            p.inc(i, 'wda', 16)
        for ii in range(4):
            i = g.dma_start(out=WOL[:, ii * 1024:(ii + 1) * 1024],
                            in_=ap['wo_l%d' % l][ii * 128:(ii + 1) * 128, :])
            p.inc(i, 'wda', 16)
        i = g.dma_start(out=QSL[:, 0:1], in_=ap['qs_l%d' % l]); p.inc(i, 'wda', 16)
        i = g.dma_start(out=QSL[:, 1:2], in_=ap['ks_l%d' % l]); p.inc(i, 'wda', 16)
        i = g.dma_start(out=LGC[:], in_=ap['lg_l%d' % l]); p.inc(i, 'wda', 16)
        i = g.dma_start(out=LBC[:], in_=ap['lb_l%d' % l]); p.inc(i, 'wda', 16)
        i = g.dma_start(out=B2L[:], in_=ap['b2_l%d' % l]); p.inc(i, 'wda', 16)
        wl = p.val('wda')
        p.wait('v', 'wda', wl)
        v.tensor_scalar(QS8[:], QSL[:, 0:1], 8.0, None, op0=ALU.mult)
        v.tensor_copy(KSC[:], QSL[:, 1:2])
        p.wait('pe', 'wda', wl)
        p.wait('a', 'wda', wl)

        for tc in range(4):
            build_xtf(tc)
            tcol = tc * 512
            p.wait('g', 'ch_v', p.val('ch_v'))
            for nm, dst in (("c_aq", AQc), ("c_bq", BQc), ("c_ak", AKc),
                            ("c_bk", BKc)):
                i = g.dma_start(out=dst[:], in_=ap[nm][:, tcol:tcol + 512])
                p.inc(i, 'dcs', 16)
            p.wait('v', 'dcs', p.val('dcs'))
            for qk in range(2):
                dst = AQFL if qk == 0 else AKFL
                atab, btab = (AQc, BQc) if qk == 0 else (AKc, BKc)
                scol = QS8 if qk == 0 else KSC
                for dt in range(4):
                    wcol = qk * 512 + dt * 128
                    for di in range(ND):
                        i = pe.matmul(B[0][:],
                                      WQKV[:, di * 1536 + wcol: di * 1536 + wcol + 128],
                                      hslice(di, 0, 512),
                                      start=(di == 0), stop=(di == ND - 1))
                    p.inc(i, 'mm')
                    p.wait('a', 'mm')
                    a.activation(out=T1f[:], in_=B[0][:], func=AF.Square)
                    a.activation(out=QRAW[:], in_=B[0][:], func=AF.Copy)
                    p.wait('pe', 'ch_a')
                    i = pe.matmul(B[1][0:2, 0:512], BLK2f[:], T1f[:],
                                  start=True, stop=True)
                    p.inc(i, 'mm')
                    p.wait('a', 'mm')
                    a.activation(out=NS[:], in_=B[1][0:2, 0:512], func=AF.Sqrt)
                    p.wait('v', 'ch_a')
                    v.tensor_scalar(NS[:], NS[:], 1e-12, None, op0=ALU.max)
                    v.reciprocal(NS[:], NS[:])
                    p.wait('pe', 'ch_v')
                    i = pe.matmul(B[2][:, 0:512], PAT2f[:], NS[:],
                                  start=True, stop=True)
                    p.inc(i, 'mm')
                    p.wait('v', 'mm')
                    v.scalar_tensor_tensor(T2b[:], QRAW[:], scol[:], B[2][:, 0:512],
                                           op0=ALU.mult, op1=ALU.mult)
                    p.wait('pe', 'ch_v')
                    i = pe.matmul(B[3][:, 0:512], PERMb[:], T2b[:],
                                  start=True, stop=True)
                    p.inc(i, 'mm')
                    v.tensor_tensor(T3b[:], T2b[:], atab[:], op=ALU.mult)
                    p.wait('v', 'mm')
                    v.tensor_tensor(T2b[:], B[3][:, 0:512], btab[:], op=ALU.mult)
                    v.tensor_tensor(dst[:, dt * 2048 + tcol: dt * 2048 + tcol + 512],
                                    T3b[:], T2b[:], op=ALU.add)
                    p.wait('pe', 'ch_v')
            for tt in range(4):
                t = tc * 4 + tt
                for di in range(ND):
                    i = pe.matmul(B[0][:],
                                  hslice(di, tt * 128, 128),
                                  WQKV[:, di * 1536 + 1024: di * 1536 + 1536],
                                  start=(di == 0), stop=(di == ND - 1))
                p.inc(i, 'mm')
                p.wait('v', 'mm')
                v.tensor_copy(AVTL[:, t * 512:(t + 1) * 512], B[0][:])
                p.wait('pe', 'ch_v')

        sc_free = {0: 0, 1: 0}
        tp_free = {4: 0, 6: 0}
        om_free = {5: 0, 7: 0}
        PTB2 = T1f[:].bitcast(BF16)
        PBUF2 = XIB
        for it in range(NT):
            kt0 = max(0, it - 4)
            nkt = it - kt0 + 1
            for h in range(8):
                par = h % 2
                SCa, SCb = B[2 * par], B[2 * par + 1]
                tb = 4 if par == 0 else 6
                vb = 5 if par == 0 else 7
                pb = PBUF if par == 0 else PBUF2
                ptb = PTB if par == 0 else PTB2
                rstd = RSTD[:] if par == 0 else MV[:, 0:1]
                nmr = NMR[:] if par == 0 else MV[:, 1:2]
                z0 = 4 * par
                hb = (h // 2) * 2048
                hp = (h % 2) * 64
                qsl = AQFL[hp:hp + 64, hb + it * 128: hb + (it + 1) * 128]
                p.wait('pe', 'ch_v', sc_free[par])
                for kk in range(nkt):
                    kt = kt0 + kk
                    ksl = AKFL[hp:hp + 64, hb + kt * 128: hb + (kt + 1) * 128]
                    bank = SCa if kk < 4 else SCb
                    i = pe.matmul(bank[:, (kk % 4) * 128:(kk % 4) * 128 + 128],
                                  qsl, ksl, start=True, stop=True)
                p.inc(i, 'scm')
                p.wait('v', 'scm')
                lastb = SCa if nkt <= 4 else SCb
                lastr = ((nkt - 1) % 4) * 128
                v.tensor_tensor(lastb[:, lastr:lastr + 128],
                                lastb[:, lastr:lastr + 128], AMTRIL[:], op=ALU.add)
                if nkt == 5:
                    v.tensor_tensor(SCa[:, 0:128], SCa[:, 0:128], AMTRIU[:],
                                    op=ALU.add)
                n1 = min(nkt, 4) * 128
                v.tensor_reduce(rstd, SCa[:, 0:n1], axis=AX.X, op=ALU.max,
                                negate=True)
                if nkt == 5:
                    v.tensor_reduce(nmr, SCb[:, 0:128], axis=AX.X, op=ALU.max,
                                    negate=True)
                    v.tensor_tensor(rstd, rstd, nmr, op=ALU.min)
                p.wait('a', 'ch_v')
                a.activation(out=pb[:, 0:n1], in_=SCa[:, 0:n1], func=AF.Exp,
                             bias=rstd, accum_out=ZC[:, z0:z0 + 1])
                if nkt == 5:
                    a.activation(out=pb[:, 512:640], in_=SCb[:, 0:128],
                                 func=AF.Exp, bias=rstd,
                                 accum_out=ZC[:, z0 + 1:z0 + 2])
                exp_mark = p.val('ch_a')
                p.wait('v', 'ch_a')
                if nkt == 5:
                    v.tensor_tensor(ZC[:, z0:z0 + 1], ZC[:, z0:z0 + 1],
                                    ZC[:, z0 + 1:z0 + 2], op=ALU.add)
                v.reciprocal(ZC[:, z0 + 2:z0 + 3], ZC[:, z0:z0 + 1])
                sc_free[par] = p.val('ch_v')
                p.wait('pe', 'ch_a', exp_mark)
                p.wait('pe', 'ch_v', tp_free[tb])
                for kk in range(nkt):
                    i = pe.transpose(B[tb][:, kk * 64:(kk + 1) * 64].bitcast(BF16),
                                     pb[:, kk * 128:(kk + 1) * 128], IDb[:])
                p.inc(i, 'tp')
                p.wait('v', 'tp')
                v.tensor_copy(ptb[:, 0:nkt * 128],
                              B[tb][:, 0:nkt * 64].bitcast(BF16))
                tp_free[tb] = p.val('ch_v')
                p.wait('pe', 'ch_v', max(tp_free[tb], om_free[vb]))
                for kk in range(nkt):
                    kt = kt0 + kk
                    i = pe.matmul(B[vb][:, 0:64],
                                  ptb[:, kk * 128:(kk + 1) * 128],
                                  AVTL[:, kt * 512 + h * 64: kt * 512 + (h + 1) * 64],
                                  start=(kk == 0), stop=(kk == nkt - 1))
                p.inc(i, 'om')
                p.wait('v', 'om')
                v.tensor_scalar(OTM[:, h * 64:(h + 1) * 64], B[vb][:, 0:64],
                                ZC[:, z0 + 2:z0 + 3], None, op0=ALU.mult)
                om_free[vb] = p.val('ch_v')
            p.wait('pe', 'ch_v', p.val('ch_v'))
            for dd in range(4):
                i = pe.transpose(B[4][:, dd * 64:(dd + 1) * 64].bitcast(BF16),
                                 OTM[:, dd * 128:(dd + 1) * 128], IDb[:])
            p.inc(i, 'tp')
            p.wait('v', 'tp')
            v.tensor_copy(OTT[:], B[4][:, 0:256].bitcast(BF16))
            p.wait('pe', 'ch_v')
            for dh in range(2):
                bank = B[4] if dh == 0 else B[5]
                for ii in range(4):
                    i = pe.matmul(bank[:],
                                  OTT[:, ii * 128:(ii + 1) * 128],
                                  WOL[:, ii * 1024 + dh * 512:
                                      ii * 1024 + (dh + 1) * 512],
                                  start=(ii == 0), stop=(ii == 3))
                p.inc(i, 'mm')
            p.wait('v', 'mm')
            for dh in range(2):
                bank = B[4] if dh == 0 else B[5]
                v.tensor_tensor(XI[:, it * D + dh * 512: it * D + (dh + 1) * 512],
                                XI[:, it * D + dh * 512: it * D + (dh + 1) * 512],
                                bank[:], op=ALU.add)
            tp_free[4] = p.val('ch_v')
            om_free[5] = p.val('ch_v')

        p.wait('a', 'ch_v', p.val('ch_v'))
        for t in range(NT):
            for half in range(2):
                xs = XI[:, t * D + half * 512: t * D + (half + 1) * 512]
                a.activation(out=T1f[:], in_=xs, func=AF.Copy,
                             accum_out=STS[:, t * 4 + half: t * 4 + half + 1])
                a.activation(out=T1f[:], in_=xs, func=AF.Square,
                             accum_out=STS[:, t * 4 + 2 + half: t * 4 + 3 + half])
        stats_done = p.val('ch_a')
        p.wait('pe', 'ch_a', stats_done)
        for t in range(NT):
            i = pe.matmul(B[0][:, 0:2], SHIFTP[:],
                          STS[:, t * 4 + 1:t * 4 + 4:2], start=True,
                          stop=(t == 0))
            if t > 0:
                i = pe.matmul(B[0][:, 0:2], CORNER[:],
                              STS[:, (t - 1) * 4 + 1:(t - 1) * 4 + 4:2],
                              start=False, stop=True)
            p.inc(i, 'mm')
            p.wait('v', 'mm')
            v.tensor_tensor(MUC[:, t:t + 1], STS[:, t * 4:t * 4 + 1],
                            B[0][:, 0:1], op=ALU.add)
            v.tensor_scalar(MUC[:, t:t + 1], MUC[:, t:t + 1], 1.0 / D, None,
                            op0=ALU.mult)
            v.tensor_tensor(MUC[:, 32 + t:33 + t], STS[:, t * 4 + 2:t * 4 + 3],
                            B[0][:, 1:2], op=ALU.add)
            v.tensor_scalar(MUC[:, 32 + t:33 + t], MUC[:, 32 + t:33 + t],
                            1.0 / D, None, op0=ALU.mult)
            v.tensor_tensor(MUC[:, 16 + t:17 + t], MUC[:, t:t + 1],
                            MUC[:, t:t + 1], op=ALU.mult)
            v.tensor_tensor(MUC[:, 16 + t:17 + t], MUC[:, 32 + t:33 + t],
                            MUC[:, 16 + t:17 + t], op=ALU.subtract)
            v.tensor_scalar(MUC[:, 16 + t:17 + t], MUC[:, 16 + t:17 + t],
                            1e-5, None, op0=ALU.add)
            p.wait('a', 'ch_v')
            a.activation(out=MUC[:, 16 + t:17 + t], in_=MUC[:, 16 + t:17 + t],
                         func=AF.Sqrt)
            p.wait('v', 'ch_a')
            v.reciprocal(MUC[:, 16 + t:17 + t], MUC[:, 16 + t:17 + t])
            p.wait('pe', 'ch_v')
        i = pe.transpose(B[1][0:32, 0:128], MUC[:, 0:32], IDf[:])
        p.inc(i, 'tp')
        p.wait('v', 'tp')
        v.tensor_copy(TRS[:], B[1][0:32, 0:128])
        mur_v = p.val('ch_v')
        p.wait('g', 'ch_v', mur_v)
        i = g.dma_start(out=MRD.ap().rearrange("a (gg b) -> (a gg) b", b=128),
                        in_=TRS[:])
        p.inc(i, 'gd', 16)

        for tc in range(4):
            build_xtf(tc)
            p.wait('g', 'mm', p.val('mm'))
            p.wait('g', 'gd', p.val('gd'))
            sl = slice(tc * 512, (tc + 1) * 512)
            i = g.dma_start(out=MUCH[:], in_=MRD.ap()[0:1, sl])
            p.inc(i, 'gd', 16)
            i = g.dma_start(out=RSCH[:], in_=MRD.ap()[1:2, sl])
            p.inc(i, 'gd', 16)
            p.wait('pe', 'gd', p.val('gd'))
            i = pe.matmul(B[0][:, 0:512], ONESF[:],
                          MUCH[:], start=True, stop=True)
            p.inc(i, 'mm')
            i = pe.matmul(B[1][:, 0:512], ONESF[:],
                          RSCH[:], start=True, stop=True)
            p.inc(i, 'mm')
            p.wait('v', 'mm')
            for di in range(ND):
                hsl = hslice(di, 0, 512)
                v.tensor_tensor(T1f[:], hsl, B[0][:, 0:512], op=ALU.subtract)
                v.tensor_tensor(T1f[:], T1f[:], B[1][:, 0:512], op=ALU.mult)
                v.tensor_scalar(AHLN[:, di * 2048 + tc * 512:
                                    di * 2048 + (tc + 1) * 512],
                                T1f[:], LGC[:, di:di + 1], LBC[:, di:di + 1],
                                op0=ALU.mult, op1=ALU.add)
            p.wait('pe', 'ch_v')

        FT = [128] * 21 + [42]
        fchunks = [list(range(0, 6)), list(range(6, 12)), list(range(12, 18)),
                   list(range(18, 22))]
        ge_marks = {0: 0, 1: 0}
        geglu_marks = {}
        w2_mark = [0]
        for ci, fts in enumerate(fchunks):
            for fi, ft in enumerate(fts):
                fw = FT[ft]
                buf = ft % 2
                wsem = 'wda' if buf == 0 else 'wdb'
                p.wait('g', 'mm', ge_marks[buf])
                p.wait('g', 'mm', w2_mark[0])
                w1r = ap['w1_l%d' % l]
                for di in range(ND):
                    i = g.dma_start(out=W1A[buf][:, di * 128: di * 128 + fw],
                                    in_=w1r[di * 128:(di + 1) * 128,
                                            ft * 128: ft * 128 + fw])
                    p.inc(i, wsem, 16)
                    i = g.dma_start(out=W1G[buf][:, di * 128: di * 128 + fw],
                                    in_=w1r[di * 128:(di + 1) * 128,
                                            2730 + ft * 128: 2730 + ft * 128 + fw])
                    p.inc(i, wsem, 16)
                i = g.dma_start(out=B1Ta[buf][0:1, 0:fw],
                                in_=ap['b1_l%d' % l][0:1, ft * 128: ft * 128 + fw])
                p.inc(i, wsem, 16)
                i = g.dma_start(out=B1Tg[buf][0:1, 0:fw],
                                in_=ap['b1_l%d' % l][0:1, 2730 + ft * 128:
                                                     2730 + ft * 128 + fw])
                p.inc(i, wsem, 16)
                i = g.dma_start(out=W2T6[0:fw, fi * 1024:(fi + 1) * 1024],
                                in_=ap['w2_l%d' % l][ft * 128: ft * 128 + fw, :])
                p.inc(i, wsem, 16)
                p.wait('pe', wsem, p.val(wsem))
                for tcc in range(4):
                    gb = 0 if tcc % 2 == 0 else 2
                    mk = geglu_marks.get(('ag', gb))
                    if mk:
                        p.wait('pe', 'ch_v', mk)
                    for gg in range(2):
                        bank = B[gb + gg]
                        wsrc = W1A if gg == 0 else W1G
                        for di in range(ND):
                            i = pe.matmul(bank[0:fw, 0:512],
                                          wsrc[buf][:, di * 128: di * 128 + fw],
                                          AHLN[:, di * 2048 + tcc * 512:
                                               di * 2048 + (tcc + 1) * 512],
                                          start=(di == 0), stop=False)
                        brow = B1Ta if gg == 0 else B1Tg
                        i = pe.matmul(bank[0:fw, 0:512],
                                      brow[buf][0:1, 0:fw],
                                      ONE512b[:],
                                      start=False, stop=True)
                        p.inc(i, 'mm')
                    if tcc == 3:
                        ge_marks[buf] = p.val('mm')
                    p.wait('a', 'mm')
                    gt = GTB if gb == 0 else GTB2
                    a.activation(out=gt[0:fw, 0:512], in_=B[gb + 1][0:fw, 0:512],
                                 func=AF.Gelu)
                    p.wait('v', 'ch_a')
                    v.tensor_tensor(AUB[0:fw, fi * 2048 + tcc * 512:
                                        fi * 2048 + (tcc + 1) * 512],
                                    B[gb][0:fw, 0:512], gt[0:fw, 0:512],
                                    op=ALU.mult)
                    geglu_marks[('ag', gb)] = p.val('ch_v')
            for t in range(NT):
                yb = 4 if t % 2 == 0 else 2
                mk = max(geglu_marks.get(('y', yb), 0),
                         geglu_marks.get(('ag', yb), 0))
                if mk:
                    p.wait('pe', 'ch_v', mk)
                for dh in range(2):
                    bank = B[yb + dh]
                    nf = len(fts)
                    for fi, ft in enumerate(fts):
                        fw = FT[ft]
                        last = (fi == nf - 1)
                        i = pe.matmul(bank[:],
                                      AUB[0:fw, fi * 2048 + t * 128:
                                          fi * 2048 + (t + 1) * 128],
                                      W2T6[0:fw, fi * 1024 + dh * 512:
                                           fi * 1024 + (dh + 1) * 512],
                                      start=(fi == 0),
                                      stop=(last and ci != 0))
                    if ci == 0:
                        i = pe.matmul(bank[:], ONESr[:],
                                      B2L[0:1, dh * 512:(dh + 1) * 512],
                                      start=False, stop=True)
                    p.inc(i, 'mm')
                p.wait('v', 'mm')
                for dh in range(2):
                    bank = B[yb + dh]
                    v.tensor_tensor(XI[:, t * D + dh * 512: t * D + (dh + 1) * 512],
                                    XI[:, t * D + dh * 512: t * D + (dh + 1) * 512],
                                    bank[:], op=ALU.add)
                geglu_marks[('y', yb)] = p.val('ch_v')
                geglu_marks[('ag', yb)] = p.val('ch_v')
            w2_mark[0] = p.val('mm')

    for t in range(NT):
        xsem = 'dy' if t % 2 == 0 else 'dz'
        p.wait('y', 'ch_v', p.val('ch_v'))
        i = y.dma_start(out=XT[:, 0:D], in_=xr[t])
        p.inc(i, xsem, 16)
        xit = XI[:, t * D:(t + 1) * D]
        emit_ln_stats(p, [xit[:, 0:512], xit[:, 512:1024]], BN, MV, RSTD, NMR)
        p.wait('v', xsem, p.val(xsem))
        v.tensor_scalar(xit, xit, RSTD[:], NMR[:], op0=ALU.mult, op1=ALU.add)
        v.tensor_tensor(xit, xit, XT[:, 0:D], op=ALU.add)
        fin = p.val('ch_v')
        p.wait('g', 'ch_v', fin)
        i = g.dma_start(out=ap['xo'].rearrange("(t p) d -> t p d", p=128)[t],
                        in_=xit)
        p.inc(i, 'xd', 16)
    p.wait('g', 'xd')


def build_l1_ffn(detect_races=True):
    nc = bass.Bass("TRN2", target_bir_lowering=False,
                   detect_race_conditions=detect_races)
    ap = {}
    ap['x'] = nc.dram_tensor("x", [T, D], F32, kind="ExternalInput").ap()
    for l in range(2):
        ap['qkv_l%d' % l] = nc.dram_tensor("qkv_l%d" % l, [D, 1536], F32,
                                           kind="ExternalInput").ap()
        ap['wo_l%d' % l] = nc.dram_tensor("wo_l%d" % l, [512, D], F32,
                                          kind="ExternalInput").ap()
        ap['w1_l%d' % l] = nc.dram_tensor("w1_l%d" % l, [D, 5460], F32,
                                          kind="ExternalInput").ap()
        ap['w2_l%d' % l] = nc.dram_tensor("w2_l%d" % l, [2730, D], F32,
                                          kind="ExternalInput").ap()
        ap['b1_l%d' % l] = nc.dram_tensor("b1_l%d" % l, [1, 5460], F32,
                                          kind="ExternalInput").ap()
        ap['b2_l%d' % l] = nc.dram_tensor("b2_l%d" % l, [1, D], F32,
                                          kind="ExternalInput").ap()
        ap['qs_l%d' % l] = nc.dram_tensor("qs_l%d" % l, [128, 1], F32,
                                          kind="ExternalInput").ap()
        ap['ks_l%d' % l] = nc.dram_tensor("ks_l%d" % l, [128, 1], F32,
                                          kind="ExternalInput").ap()
        ap['lg_l%d' % l] = nc.dram_tensor("lg_l%d" % l, [128, 8], F32,
                                          kind="ExternalInput").ap()
        ap['lb_l%d' % l] = nc.dram_tensor("lb_l%d" % l, [128, 8], F32,
                                          kind="ExternalInput").ap()
    cc = host_consts()
    cc["c_idf"] = cc["c_id"]
    for n, arr in cc.items():
        if n == "c_idf":
            ap[n] = nc.dram_tensor("c_idf", [128, 128], F32, kind="ExternalInput").ap()
        else:
            ap[n] = nc.dram_tensor(n, list(arr.shape), F32, kind="ExternalInput").ap()
    ap['xo'] = nc.dram_tensor("xo", [T, D], F32, kind="ExternalOutput").ap()

    with ExitStack() as st:
        def sb(name, shape, dt):
            return st.enter_context(nc.sbuf_tensor(name, shape, dt))

        def ps(name, shape, dt):
            return st.enter_context(nc.psum_tensor(name, shape, dt))

        sb_ = alloc_p4(sb, ps, nc.dram_tensor("MRD", [2, 2048], F32))

        run_script(nc, lambda p: emit_l1_ffn(p, ap, sb_), SEM_NAMES)
    return nc


def alloc_p4(sb, ps, mrd):
    sb_ = {}
    sb_['MRD'] = mrd
    sb_['XI'] = sb("XI", [128, NT * D], F32)
    sb_['AR'] = sb("AR", [128, 28704], BF16)
    sb_['XT'] = sb("XT", [128, D], F32)
    sb_['WQKV'] = sb("WQKV", [128, ND * 1536], BF16)
    sb_['WOL'] = sb("WOL", [128, 4 * 1024], BF16)
    sb_['W1A'] = [sb(f"W1A{i}", [128, ND * 128], BF16) for i in range(2)]
    sb_['W1G'] = [sb(f"W1G{i}", [128, ND * 128], BF16) for i in range(2)]
    sb_['W2T6'] = sb("W2T6", [128, 6 * 1024], BF16)
    sb_['B1Ta'] = [sb(f"B1Ta{i}", [1, 128], BF16) for i in range(2)]
    sb_['B1Tg'] = [sb(f"B1Tg{i}", [1, 128], BF16) for i in range(2)]
    sb_['B2L'] = sb("B2L", [1, 1024], BF16)
    sb_['AQc'] = sb("AQc", [128, 512], BF16)
    sb_['BQc'] = sb("BQc", [128, 512], BF16)
    sb_['AKc'] = sb("AKc", [128, 512], BF16)
    sb_['BKc'] = sb("BKc", [128, 512], BF16)
    sb_['PERMb'] = sb("PERMb", [128, 128], BF16)
    sb_['SHIFTP'] = sb("SHIFTPc", [128, 128], F32)
    sb_['CORNER'] = sb("CORNERc", [128, 128], F32)
    sb_['AMTRIL'] = sb("AMTRIL", [128, 128], F32)
    sb_['AMTRIU'] = sb("AMTRIU", [128, 128], F32)
    sb_['BLK2f'] = sb("BLK2f", [128, 2], F32)
    sb_['PAT2f'] = sb("PAT2f", [2, 128], F32)
    sb_['ONESF'] = sb("ONESF", [1, 128], F32)
    sb_['ONESr'] = sb("ONESrb", [1, 128], BF16)
    sb_['ONE512b'] = sb("ONE512b", [1, 512], BF16)
    sb_['IDb'] = sb("IDb", [128, 128], BF16)
    sb_['IDf'] = sb("IDf", [128, 128], F32)
    sb_['QS8'] = sb("QS8", [128, 1], F32)
    sb_['KSC'] = sb("KSC", [128, 1], F32)
    sb_['QSL'] = sb("QSL", [128, 2], F32)
    sb_['LGC'] = sb("LGC", [128, 8], F32)
    sb_['LBC'] = sb("LBC", [128, 8], F32)
    sb_['T1f'] = sb("T1f", [128, 512], F32)
    sb_['QRAW'] = sb("QRAW", [128, 512], BF16)
    sb_['NS'] = sb("NS", [2, 512], F32)
    sb_['PBUF'] = sb("PBUF", [128, 640], BF16)
    sb_['PTB'] = sb("PTB", [128, 640], BF16)
    sb_['OTM'] = sb("OTM", [128, 512], BF16)
    sb_['OTT'] = sb("OTT", [128, 512], BF16)
    sb_['STS'] = sb("STS", [128, 64], F32)
    sb_['MUC'] = sb("MUC", [128, 48], F32)
    sb_['TRS'] = sb("TRS", [32, 128], F32)
    sb_['MUCH'] = sb("MUCH", [1, 512], F32)
    sb_['RSCH'] = sb("RSCH", [1, 512], F32)
    sb_['LASTC'] = sb("LASTC", [128, 8], BF16)
    sb_['ZC'] = sb("ZC", [128, 8], F32)
    sb_['GTB'] = sb("GTB", [128, 512], BF16)
    sb_['GTB2'] = sb("GTB2", [128, 512], BF16)
    sb_['XIB'] = sb("XIB", [128, D], BF16)
    sb_['BN'] = sb("BN", [128, 12], F32)
    sb_['MV'] = sb("MV", [128, 2], F32)
    sb_['RSTD'] = sb("RSTD", [128, 1], F32)
    sb_['NMR'] = sb("NMR", [128, 1], F32)
    sb_['B'] = [ps(f"bank{i}", [128, 512], F32) for i in range(8)]
    return sb_




class PhasedAlloc:
    def __init__(self, nc):
        self.nc = nc
        self.cache = {}
        self.live = []

    def sbf(self, pref):
        def f(name, shape, dt):
            key = pref + "_" + name
            if key not in self.cache:
                cm = self.nc.sbuf_tensor(key, shape, dt)
                self.cache[key] = cm.__enter__()
                self.live.append(cm)
            return self.cache[key]
        return f

    def psf(self, pref):
        def f(name, shape, dt):
            key = pref + "_" + name
            if key not in self.cache:
                cm = self.nc.psum_tensor(key, shape, dt)
                self.cache[key] = cm.__enter__()
                self.live.append(cm)
            return self.cache[key]
        return f

    def end_phase(self):
        for cm in reversed(self.live):
            cm.__exit__(None, None, None)
        self.live = []


def emit_phase_barrier(p):
    for e in ('pe', 'v', 'a', 'g', 'y'):
        for s_ in ('ch_v', 'ch_a', 'ch_g', 'mm', 'tp', 'scm', 'om', 'sm',
                   'cc', 'wd'):
            p.wait(e, s_, p.val(s_))
    p.wait('y', 'xd', p.val('xd'))
    p.wait('g', 'xd', p.val('xd'))


def build_fused(detect_races=True):
    nc = bass.Bass("TRN2", target_bir_lowering=False,
                   detect_race_conditions=detect_races)
    ap_c = {}
    cc = host_consts()
    cc["c_idf"] = cc["c_id"]
    for n, arr in cc.items():
        ap_c[n] = nc.dram_tensor(n, list(arr.shape), F32,
                                 kind="ExternalInput").ap()

    def din(name, shape):
        return nc.dram_tensor(name, shape, F32, kind="ExternalInput").ap()

    x_in = din("x", [T, D])
    ap1 = dict(ap_c, x=x_in,
               Wq=din("p1_Wq", [D, D]), Wk=din("p1_Wk", [D, D]),
               Wv=din("p1_Wv", [D, D]), Wo=din("p1_Wo", [D, D]),
               bo=din("p1_bo", [1, D]))
    ap2 = dict(ap_c,
               Wr=din("p2_Wr", [D, 8]), br=din("p2_br", [1, 8]),
               W1=din("p2_W1", [8, D, 4096]), b1=din("p2_b1", [8, 4096]),
               W2=din("p2_W2", [8, 4096, D]), b2=din("p2_b2", [8, D]))
    ap3 = dict(ap_c,
               Wq=din("p3_Wq", [D, D]), Wkv=din("p3_Wkv", [D, 2048]),
               Wo=din("p3_Wo", [D, D]))
    ap4 = dict(ap_c)
    for l in range(2):
        ap4['qkv_l%d' % l] = din("qkv_l%d" % l, [D, 1536])
        ap4['wo_l%d' % l] = din("wo_l%d" % l, [512, D])
        ap4['w1_l%d' % l] = din("w1_l%d" % l, [D, 5460])
        ap4['w2_l%d' % l] = din("w2_l%d" % l, [2730, D])
        ap4['b1_l%d' % l] = din("b1_l%d" % l, [1, 5460])
        ap4['b2_l%d' % l] = din("b2_l%d" % l, [1, D])
        ap4['qs_l%d' % l] = din("qs_l%d" % l, [128, 1])
        ap4['ks_l%d' % l] = din("ks_l%d" % l, [128, 1])
        ap4['lg_l%d' % l] = din("lg_l%d" % l, [128, 8])
        ap4['lb_l%d' % l] = din("lb_l%d" % l, [128, 8])

    Y1 = nc.dram_tensor("Y1", [T, D], F32).ap()
    Y2 = nc.dram_tensor("Y2", [T, D], F32).ap()
    Y3 = nc.dram_tensor("Y3", [T, D], F32).ap()
    xo = nc.dram_tensor("xo", [T, D], F32, kind="ExternalOutput").ap()
    ap1['xo'] = Y1
    ap2['x'] = Y1
    ap2['xo'] = Y2
    ap3['x'] = Y2
    ap3['xo'] = Y3
    ap4['x'] = Y3
    ap4['xo'] = xo

    gl = declare_p3_dram(nc)
    mrd = nc.dram_tensor("MRD", [2, 2048], F32)

    A = PhasedAlloc(nc)

    def script(p):
        sb1 = alloc_p1(A.sbf('p1'), A.psf('p1'))
        emit_l0_attn(p, ap1, sb1)
        emit_store_X(p, sb1, Y1)
        A.end_phase()
        emit_phase_barrier(p)

        sb2 = alloc_p2(A.sbf('p2'), A.psf('p2'))
        emit_l0_ffn(p, ap2, sb2)
        A.end_phase()
        emit_phase_barrier(p)

        sb3 = alloc_p3(A.sbf('p3'), A.psf('p3'), gl)
        emit_l1_attn(p, ap3, sb3)
        A.end_phase()
        emit_phase_barrier(p)

        sb4 = alloc_p4(A.sbf('p4'), A.psf('p4'), mrd)
        emit_l1_ffn(p, ap4, sb4)
        A.end_phase()

    run_script(nc, script, SEM_NAMES)
    return nc


def fused_in_maps(inputs):
    consts = host_consts()
    in_maps = []
    for c in range(8):
        m = {"x": np.ascontiguousarray(
                inputs["L0_attn_x"][c].astype(np.float32)),
             "p1_Wq": inputs["L0_attn_Wq"], "p1_Wk": inputs["L0_attn_Wk"],
             "p1_Wv": inputs["L0_attn_Wv"], "p1_Wo": inputs["L0_attn_Wo"],
             "p1_bo": inputs["L0_attn_bo"].reshape(1, -1),
             "p2_Wr": inputs["L0_ffn_Wr"],
             "p2_br": inputs["L0_ffn_br"].reshape(1, -1),
             "p2_W1": inputs["L0_ffn_W1"], "p2_b1": inputs["L0_ffn_b1"],
             "p2_W2": inputs["L0_ffn_W2"], "p2_b2": inputs["L0_ffn_b2"],
             "p3_Wq": inputs["L1_attn_Wq"], "p3_Wkv": inputs["L1_attn_Wkv"],
             "p3_Wo": inputs["L1_attn_Wo"]}
        for l in range(2):
            m[f"qkv_l{l}"] = inputs["L1_ffn_wqkv"][l]
            m[f"wo_l{l}"] = inputs["L1_ffn_wo"][l]
            m[f"w1_l{l}"] = inputs["L1_ffn_w1"][l]
            m[f"w2_l{l}"] = inputs["L1_ffn_w2"][l]
            m[f"b1_l{l}"] = inputs["L1_ffn_b1"][l].reshape(1, -1)
            m[f"b2_l{l}"] = inputs["L1_ffn_b2"][l].reshape(1, -1)
            m[f"qs_l{l}"] = np.ascontiguousarray(
                np.concatenate([inputs["L1_ffn_q_scale"][l]] * 2).reshape(128, 1))
            m[f"ks_l{l}"] = np.ascontiguousarray(
                np.concatenate([inputs["L1_ffn_k_scale"][l]] * 2).reshape(128, 1))
            m[f"lg_l{l}"] = np.ascontiguousarray(
                inputs["L1_ffn_ln_g"][l].reshape(8, 128).T)
            m[f"lb_l{l}"] = np.ascontiguousarray(
                inputs["L1_ffn_ln_b"][l].reshape(8, 128).T)
        m.update(consts)
        m["c_idf"] = consts["c_id"]
        in_maps.append(m)
    return in_maps



_NC_CACHE = {}


def kernel(**inputs):
    if 'fused' not in _NC_CACHE:
        _NC_CACHE['fused'] = build_fused(detect_races=False)
    nc = _NC_CACHE['fused']
    in_maps = fused_in_maps(inputs)
    res = run_bass_kernel_spmd(nc, in_maps, core_ids=list(range(8)))
    out = np.stack([np.asarray(r["xo"]) for r in res.results], axis=0)
    return out.astype(np.float32)


def kernel_multilaunch(**inputs):
    consts = host_consts()
    x = np.ascontiguousarray(inputs["L0_attn_x"].astype(np.float32))

    def launch(nc, mk):
        in_maps = []
        for c in range(8):
            m = mk(c)
            m.update(consts)
            in_maps.append(m)
        res = run_bass_kernel_spmd(nc, in_maps, core_ids=list(range(8)))
        return np.stack([np.asarray(r["xo"]) for r in res.results], axis=0)

    nc = build_l0_attn()
    x = launch(nc, lambda c: {
        "x": np.ascontiguousarray(x[c]),
        "Wq": inputs["L0_attn_Wq"], "Wk": inputs["L0_attn_Wk"],
        "Wv": inputs["L0_attn_Wv"], "Wo": inputs["L0_attn_Wo"],
        "bo": inputs["L0_attn_bo"].reshape(1, -1)})

    nc = build_l0_ffn()
    x = launch(nc, lambda c: {
        "x": np.ascontiguousarray(x[c]),
        "Wr": inputs["L0_ffn_Wr"], "br": inputs["L0_ffn_br"].reshape(1, -1),
        "W1": inputs["L0_ffn_W1"], "b1": inputs["L0_ffn_b1"],
        "W2": inputs["L0_ffn_W2"], "b2": inputs["L0_ffn_b2"]})

    nc = build_l1_attn()
    x = launch(nc, lambda c: {
        "x": np.ascontiguousarray(x[c]),
        "Wq": inputs["L1_attn_Wq"], "Wkv": inputs["L1_attn_Wkv"],
        "Wo": inputs["L1_attn_Wo"]})

    nc = build_l1_ffn(detect_races=False)

    def mk4(c):
        m = {"x": np.ascontiguousarray(x[c]), "c_idf": consts["c_id"]}
        for l in range(2):
            m[f"qkv_l{l}"] = inputs["L1_ffn_wqkv"][l]
            m[f"wo_l{l}"] = inputs["L1_ffn_wo"][l]
            m[f"w1_l{l}"] = inputs["L1_ffn_w1"][l]
            m[f"w2_l{l}"] = inputs["L1_ffn_w2"][l]
            m[f"b1_l{l}"] = inputs["L1_ffn_b1"][l].reshape(1, -1)
            m[f"b2_l{l}"] = inputs["L1_ffn_b2"][l].reshape(1, -1)
            m[f"qs_l{l}"] = np.ascontiguousarray(
                np.concatenate([inputs["L1_ffn_q_scale"][l]] * 2).reshape(128, 1))
            m[f"ks_l{l}"] = np.ascontiguousarray(
                np.concatenate([inputs["L1_ffn_k_scale"][l]] * 2).reshape(128, 1))
            m[f"lg_l{l}"] = np.ascontiguousarray(
                inputs["L1_ffn_ln_g"][l].reshape(8, 128).T)
            m[f"lb_l{l}"] = np.ascontiguousarray(
                inputs["L1_ffn_ln_b"][l].reshape(8, 128).T)
        return m
    x = launch(nc, mk4)
    return x.astype(np.float32)
